# version 08
# speedup vs baseline: 1.2788x; 1.0302x over previous
import math
import numpy as np
import ml_dtypes
import concourse.bass as bass
import concourse.mybir as mybir
from concourse.bass_utils import run_bass_kernel_spmd

F32 = mybir.dt.float32
BF16 = mybir.dt.bfloat16
I32 = mybir.dt.int32
AF = mybir.ActivationFunctionType
ALU = mybir.AluOpType
AX = mybir.AxisListType

T = 2048
C = 1024
NCT = C // 128
NTT = T // 128
TB = 512
NTB = T // TB

ENGINES = ("t", "v", "s", "g", "y")
DMA_QUEUES = ("y", "g", "s")


class Op:
    __slots__ = ("eng", "fn", "is_dma", "seq", "deps", "mark", "mark_idx", "tag",
                 "slot")

    def __init__(self, eng, fn, is_dma, seq, tag=""):
        self.eng = eng
        self.fn = fn
        self.is_dma = is_dma
        self.seq = seq
        self.deps = []
        self.mark = False
        self.mark_idx = 0
        self.tag = tag


class Prog:

    def __init__(self):
        self.ops = {e: [] for e in ENGINES}
        self.all_ops = []
        self.last_writer = {}
        self.readers = {}
        self.nfence = 0
        self.psum_keys = set()

    def fence(self):
        pend = {}
        for o in self.last_writer.values():
            pend[id(o)] = o
        for rs in self.readers.values():
            for o in rs:
                pend[id(o)] = o
        self.nfence += 1
        fk = ("__fence__", self.nfence)
        fops = []
        for e in ENGINES:
            o = Op(e, lambda eng: eng.nop(), False, len(self.ops[e]), "fence")
            for d in pend.values():
                if d.eng == e and not d.is_dma:
                    continue
                o.deps.append(d)
            self.ops[e].append(o)
            self.all_ops.append(o)
            fops.append(o)
        self.last_writer = {fk: fops[0]}
        self.readers = {fk: fops}
        self.fence_key = fk

    def op(self, eng, fn, reads=(), writes=(), dma=False, tag=""):
        assert eng in ENGINES
        if dma:
            assert eng in DMA_QUEUES
        o = Op(eng, fn, dma, len(self.ops[eng]), tag)
        deps = {}
        fk = getattr(self, "fence_key", None)
        if fk is not None:
            for r in self.readers.get(fk, ()):
                deps[id(r)] = r
        for k in reads:
            w = self.last_writer.get(k)
            if w is not None:
                deps[id(w)] = w
            if k in self.psum_keys and eng in ("v", "s"):
                for r in self.readers.get(k, ()):
                    if {r.eng, eng} == {"v", "s"}:
                        deps[id(r)] = r
        for k in writes:
            w = self.last_writer.get(k)
            if w is not None:
                deps[id(w)] = w
            for r in self.readers.get(k, ()):
                deps[id(r)] = r
        for d in deps.values():
            if d is o:
                continue
            if d.eng == eng and not d.is_dma and not dma:
                continue
            o.deps.append(d)
        for k in reads:
            self.readers.setdefault(k, []).append(o)
        for k in writes:
            self.last_writer[k] = o
            self.readers[k] = []
        self.ops[eng].append(o)
        self.all_ops.append(o)
        return o

    def dma(self, q, out, in_, reads=(), writes=(), tag="", **kw):
        return self.op(q, lambda e: e.dma_start(out, in_, **kw),
                       reads=reads, writes=writes, dma=True, tag=tag)

    def mm(self, out, lhsT, rhs, reads=(), writes=(), start=True, stop=True,
           tile_position=None, tag=""):
        return self.op("t", lambda e: e.matmul(out, lhsT, rhs, start=start,
                                               stop=stop, tile_position=tile_position),
                       reads=reads, writes=writes, tag=tag)

    def trans(self, out, in_, ident, reads=(), writes=(), tag=""):
        return self.op("t", lambda e: e.transpose(out, in_, ident),
                       reads=reads, writes=writes, tag=tag)

    def act(self, out, in_, func, reads=(), writes=(), bias=None, scale=None,
            accum_out=None, tag=""):
        kw = {}
        if bias is not None:
            kw["bias"] = bias
        if scale is not None:
            kw["scale"] = scale
        if accum_out is not None:
            kw["accum_out"] = accum_out
        return self.op("s", lambda e: e.activation(out, in_, func, **kw),
                       reads=reads, writes=writes, tag=tag)

    def copy(self, eng, out, in_, reads=(), writes=(), tag=""):
        if eng == "s":
            return self.op("s", lambda e: e.copy(out, in_), reads=reads,
                           writes=writes, tag=tag)
        return self.op(eng, lambda e: e.tensor_copy(out, in_), reads=reads,
                       writes=writes, tag=tag)

    def tt(self, eng, out, a, b, alu, reads=(), writes=(), tag=""):
        return self.op(eng, lambda e: e.tensor_tensor(out, a, b, alu),
                       reads=reads, writes=writes, tag=tag)

    def ts(self, eng, out, in0, s1, s2, op0, op1=None, accum_out=None,
           reads=(), writes=(), tag=""):
        kw = {}
        if op1 is not None:
            kw["op1"] = op1
        if accum_out is not None:
            kw["accum_out"] = accum_out
        return self.op(eng, lambda e: e.tensor_scalar(out, in0, s1, s2, op0, **kw),
                       reads=reads, writes=writes, tag=tag)

    def stt(self, eng, out, in0, scalar, in1, op0, op1, reads=(), writes=(), tag=""):
        return self.op(eng, lambda e: e.scalar_tensor_tensor(out, in0, scalar, in1, op0, op1),
                       reads=reads, writes=writes, tag=tag)

    def red(self, out, in_, alu, negate=False, reads=(), writes=(), tag=""):
        return self.op("v", lambda e: e.tensor_reduce(out, in_, AX.X, alu, negate=negate),
                       reads=reads, writes=writes, tag=tag)

    def scan(self, out, d0, d1, init, op0, op1, reads=(), writes=(), tag=""):
        return self.op("v", lambda e: e.tensor_tensor_scan(out, d0, d1, init, op0, op1),
                       reads=reads, writes=writes, tag=tag)

    def memset(self, eng, ap, val, reads=(), writes=(), tag=""):
        return self.op(eng, lambda e: e.memset(ap, val), reads=reads,
                       writes=writes, tag=tag)

    def recip(self, out, in_, reads=(), writes=(), tag=""):
        return self.op("v", lambda e: e.reciprocal(out, in_), reads=reads,
                       writes=writes, tag=tag)

    NS = 4

    def emit(self, nc, block, sems):
        NS = Prog.NS
        for o in self.all_ops:
            for d in o.deps:
                d.mark = True
        for e in ENGINES:
            for o in self.ops[e]:
                if o.is_dma:
                    o.mark = True
        csem_cnt = {e: 0 for e in ENGINES}
        dma_cnt = {e: 0 for e in ENGINES}
        for e in ENGINES:
            for o in self.ops[e]:
                if o.is_dma:
                    i = dma_cnt[e]
                    dma_cnt[e] += 1
                    o.slot = i % NS
                    o.mark_idx = i // NS + 1
                elif o.mark:
                    csem_cnt[e] += 1
                    o.slot = None
                    o.mark_idx = csem_cnt[e]
        streams = [(e, "c") for e in ENGINES] + \
                  [(e, s) for e in DMA_QUEUES for s in range(NS)]
        clock = {e: {s: 0 for s in streams} for e in ENGINES}
        self.waits = {}
        opclock = {}
        for o in self.all_ops:
            e = o.eng
            cl = clock[e]
            need = {}
            for d in o.deps:
                s = (d.eng, d.slot if d.is_dma else "c")
                v = d.mark_idx
                if v > cl[s] and v > need.get(s, 0):
                    need[s] = v
                dcl = opclock.get(id(d))
                if dcl:
                    for s2, v2 in dcl.items():
                        if v2 > cl[s2]:
                            cl[s2] = v2
            w = []
            for s, v in need.items():
                w.append((s[0], s[1], v))
                cl[s] = max(cl[s], v)
            if w:
                self.waits[id(o)] = w
            myclock = dict(cl)
            if o.mark:
                s_self = (e, o.slot if o.is_dma else "c")
                myclock[s_self] = max(myclock[s_self], o.mark_idx)
            opclock[id(o)] = myclock
        final_occ = {}
        for e in DMA_QUEUES:
            for s in range(NS):
                n = sum(1 for o in self.ops[e] if o.is_dma and o.slot == s)
                final_occ[(e, s)] = n
        nwaits = 0
        for e in ENGINES:
            def body(eng, e=e):
                for o in self.ops[e]:
                    for (fe, kind, v) in self.waits.get(id(o), ()):
                        sem = sems[(fe, kind)]
                        eng.wait_ge(sem, v * (16 if kind != "c" else 1))
                    if o.is_dma and o.mark_idx > 1:
                        eng.wait_ge(sems[(e, o.slot)], 16 * (o.mark_idx - 1))
                    inst = o.fn(eng)
                    if o.is_dma:
                        inst.then_inc(sems[(e, o.slot)], 16)
                    elif o.mark:
                        sem = sems[(e, "c")]
                        if e == "t":
                            inst.then_inc(sem, 1)
                        elif e in ("g", "v"):
                            eng.engine_nop().then_inc(sem, 1)
                        else:
                            eng.drain().then_inc(sem, 1)
            def body2(eng, e=e):
                body(eng, e)
                if e in DMA_QUEUES:
                    for s in range(NS):
                        if final_occ[(e, s)]:
                            eng.wait_ge(sems[(e, s)], 16 * final_occ[(e, s)])
            getattr(block, {"t": "tensor", "v": "vector", "s": "scalar",
                            "g": "gpsimd", "y": "sync"}[e])(body2)
            nwaits += sum(len(self.waits.get(id(o), ())) for o in self.ops[e])
        self.stats = {
            "ops": {e: len(self.ops[e]) for e in ENGINES},
            "waits": nwaits,
        }



def make_consts():
    c = {}
    c["ident"] = np.eye(128, dtype=ml_dtypes.bfloat16)
    c["ident_f"] = np.eye(128, dtype=np.float32)
    c["ones_col"] = np.ones((128, 1), np.float32)
    c["ones_row"] = np.ones((1, 128), np.float32)
    c["ones_row_b"] = np.ones((1, 128), dtype=ml_dtypes.bfloat16)
    dh = 64
    i = np.arange(1, dh // 2 + 1, dtype=np.float64)
    thetas = np.repeat(1000.0 ** (-2.0 * i / dh), 2)
    pos = np.arange(1, T + 1, dtype=np.float64)
    args = pos[:, None] * thetas[None, :]
    sinT = np.sin(args).T.astype(np.float32)
    cosT = np.cos(args).T.astype(np.float32)
    c["l0_sin"] = np.concatenate([sinT, sinT], 0)
    c["l0_cos"] = np.concatenate([cosT, cosT], 0)
    perm = np.zeros((128, 128), np.float32)
    for blk in range(2):
        b0 = blk * 64
        for ii in range(32):
            perm[b0 + 2 * ii + 1, b0 + 2 * ii] = -1.0
            perm[b0 + 2 * ii, b0 + 2 * ii + 1] = 1.0
    c["l0_perm"] = perm.astype(ml_dtypes.bfloat16)
    dh_i = 64
    half = np.arange(0, dh_i, 2, dtype=np.float64)
    inv_freq = 1.0 / 10000.0 ** (half / dh_i)
    tpos = np.arange(T, dtype=np.float64)
    freqs = tpos[:, None] * inv_freq[None, :]
    freqs = np.concatenate([freqs, freqs], -1)
    base_scale = (half + 0.4 * dh_i) / (1.4 * dh_i)
    power = (tpos - T // 2) / (256 // 2)
    xsc = base_scale[None, :] ** power[:, None]
    xsc = np.concatenate([xsc, xsc], -1)
    cq = (np.cos(freqs) * xsc).T; sq_ = (np.sin(freqs) * xsc).T
    ck = (np.cos(freqs) / xsc).T; sk = (np.sin(freqs) / xsc).T
    for nm, tab in (("cosq", cq), ("sinq", sq_), ("cosk", ck), ("sink", sk)):
        tt_ = np.concatenate([tab, tab], 0).astype(ml_dtypes.bfloat16)
        c[nm] = np.ascontiguousarray(tt_)
    iperm = np.zeros((128, 128), np.float32)
    for blk in range(2):
        b0 = blk * 64
        for d in range(32):
            iperm[b0 + d + 32, b0 + d] = -1.0
            iperm[b0 + d, b0 + d + 32] = 1.0
    c["in_perm"] = iperm.astype(ml_dtypes.bfloat16)
    c["in_perm_f"] = iperm.astype(np.float32)
    hm = np.zeros((128, NCT * 16), np.float32)
    bm = np.zeros((16, NCT * 128), np.float32)
    for i in range(NCT):
        hm[0:64, 16 * i + 2 * i] = 1.0
        hm[64:128, 16 * i + 2 * i + 1] = 1.0
        bm[2 * i, 128 * i:128 * i + 64] = 1.0
        bm[2 * i + 1, 128 * i + 64:128 * i + 128] = 1.0
    c["hmask"] = hm.astype(ml_dtypes.bfloat16)
    c["bcmask"] = bm.astype(np.float32)
    jj, ii = np.meshgrid(np.arange(128), np.arange(128), indexing="ij")
    c["mask_lo"] = (jj >= ii).astype(ml_dtypes.bfloat16)
    c["mask_ca"] = (jj <= ii).astype(ml_dtypes.bfloat16)
    BIGN = -4.0e36
    i2, j2 = np.meshgrid(np.arange(128), np.arange(384), indexing="ij")
    c["mband"] = np.where((j2 >= i2) & (j2 <= i2 + 256), 0.0, BIGN).astype(np.float32)
    i3, j3 = np.meshgrid(np.arange(128), np.arange(256), indexing="ij")
    c["mq1"] = np.where(j3 <= i3 + 128, 0.0, BIGN).astype(np.float32)
    i4, j4 = np.meshgrid(np.arange(128), np.arange(128), indexing="ij")
    c["mq0"] = np.where(j4 <= i4, 0.0, BIGN).astype(np.float32)
    c["scal"] = np.tile(np.asarray(sorted(SCAL_VALUES), np.float32)[None, :],
                        (128, 1))
    return c


SCAL_VALUES = {}

def scal_const(v):
    v = float(v)
    if v not in SCAL_VALUES:
        SCAL_VALUES[v] = len(SCAL_VALUES)
    return v

for _v in (1e-5, math.log(1e-6)):
    scal_const(_v)



class Ctx:

    def __init__(self, nc, P, es):
        self.nc = nc
        self.P = P
        self.es = es
        self.tiles = {}
        self.drams = {}

    def sb(self, name, shape, dtype, side=None):
        cm = self.nc.sbuf_tensor(name, list(shape), dtype, side=side)
        t = cm.__enter__()
        self.cms = getattr(self, "cms", {})
        self.cms[name] = cm
        self.es.callback(lambda n=name: self.free(n))
        self.tiles[name] = t
        return t

    def free(self, name):
        cm = self.cms.pop(name, None)
        if cm is not None:
            cm.__exit__(None, None, None)

    def ps(self, name, shape, dtype):
        cm = self.nc.psum_tensor(name, list(shape), dtype)
        t = cm.__enter__()
        self.cms = getattr(self, "cms", {})
        self.cms[name] = cm
        self.es.callback(lambda n=name: self.free(n))
        self.tiles[name] = t
        return t

    def need_consts(self, names):
        for name in names:
            if name in self.consts:
                continue
            arr = self.const_arrays[name]
            dt = {np.dtype(np.float32): F32, np.dtype(ml_dtypes.bfloat16): BF16,
                  np.dtype(np.int32): I32}[arr.dtype]
            h = self.nc.dram_tensor("c_" + name, list(arr.shape), dt,
                                    kind="ExternalInput")
            tile = self.sb("ct_" + name, list(arr.shape), dt, side="right")
            self.P.dma("y", tile[:], h.ap(), writes=["c_" + name], tag="ldconst")
            self.consts[name] = tile
            self.used_consts.append(name)

    def free_const(self, name):
        if name in self.consts:
            del self.consts[name]
            self.free("ct_" + name)

    def scal(self, v):
        cols = sorted(SCAL_VALUES)
        return self.consts["scal"][:, cols.index(float(v)):cols.index(float(v)) + 1]

    def dram(self, name, shape, dtype, kind=None):
        if name in self.drams:
            return self.drams[name]
        if kind is None:
            t = self.nc.dram_tensor(name, list(shape), dtype)
        else:
            t = self.nc.dram_tensor(name, list(shape), dtype, kind=kind)
        self.drams[name] = t
        return t


def emit_load_act(ctx, tiles, hbm, key, hname=None, q="y", tag="ld"):
    P = ctx.P
    hr = [("H", hname)] if hname else []
    for i in range(NCT):
        P.dma(q, tiles[i][:], hbm[128 * i:128 * (i + 1), :],
              reads=hr, writes=[(key, i)], tag=tag)


def emit_store_act(ctx, tiles, hbm, key, hname, q="y", tag="st"):
    P = ctx.P
    for i in range(NCT):
        P.dma(q, hbm[128 * i:128 * (i + 1), :], tiles[i][:],
              reads=[(key, i)], writes=[("H", hname)], tag=tag)


def emit_layernorm(ctx, x_tiles, xkey, out_tiles, okey, pref,
                   gamma=None, beta=None, gkey="c_gb"):
    P, cs = ctx.P, ctx.consts
    tl = ctx.tiles
    sq = tl[pref + "sq"]
    mu_r, rstd_r, t1 = tl[pref + "mu"], tl[pref + "rstd"], tl[pref + "t1"]
    muB, rstdB = tl[pref + "muB"], tl[pref + "rstdB"]
    ps0, ps1, bc = tl[pref + "ps0"], tl[pref + "ps1"], tl[pref + "bc"]
    for b in range(NTB):
        sl = slice(TB * b, TB * (b + 1))
        for i in range(NCT):
            P.mm(ps0[0:1, 0:TB], cs["ones_col"][:, 0:1], x_tiles[i][:, sl],
                 start=(i == 0), stop=(i == NCT - 1),
                 reads=[(xkey, i), "c_ones_col"], writes=[(pref + "ps0", b)],
                 tag="ln_sum")
        for i in range(NCT):
            P.tt("v", sq[:, sl], x_tiles[i][:, sl], x_tiles[i][:, sl], ALU.mult,
                 reads=[(xkey, i)], writes=[(pref + "sq", b)], tag="ln_sq")
            P.mm(ps1[0:1, 0:TB], cs["ones_col"][:, 0:1], sq[:, sl],
                 start=(i == 0), stop=(i == NCT - 1),
                 reads=[(pref + "sq", b), "c_ones_col"], writes=[(pref + "ps1", b)],
                 tag="ln_sumsq")
        P.ts("v", mu_r[0:1, sl], ps0[0:1, 0:TB], 1.0 / C, None, ALU.mult,
             reads=[(pref + "ps0", b)], writes=[(pref + "mu", b)], tag="ln_mu")
        P.tt("v", t1[0:1, sl], mu_r[0:1, sl], mu_r[0:1, sl], ALU.mult,
             reads=[(pref + "mu", b)], writes=[(pref + "t1", b)], tag="ln_mu2")
        P.stt("v", t1[0:1, sl], ps1[0:1, 0:TB], 1.0 / C, t1[0:1, sl],
              ALU.mult, ALU.subtract,
              reads=[(pref + "ps1", b), (pref + "t1", b)], writes=[(pref + "t1", b)],
              tag="ln_var")
        P.act(t1[0:1, sl], t1[0:1, sl], AF.Sqrt, bias=ctx.scal(1e-5)[0:1, :],
              reads=[(pref + "t1", b), "c_scal"], writes=[(pref + "t1", b)],
              tag="ln_std")
        P.recip(rstd_r[0:1, sl], t1[0:1, sl],
                reads=[(pref + "t1", b)], writes=[(pref + "rstd", b)], tag="ln_rstd")
        P.mm(bc[:, 0:TB], cs["ones_row"][0:1, :], mu_r[0:1, sl],
             reads=[(pref + "mu", b), "c_ones_row"], writes=[(pref + "bc", b)],
             tag="ln_bc_mu")
        P.copy("v", muB[:, sl], bc[:, 0:TB],
               reads=[(pref + "bc", b)], writes=[(pref + "muB", b)], tag="ln_muB")
        P.mm(bc[:, 0:TB], cs["ones_row"][0:1, :], rstd_r[0:1, sl],
             reads=[(pref + "rstd", b), "c_ones_row"], writes=[(pref + "bc", b)],
             tag="ln_bc_r")
        P.copy("v", rstdB[:, sl], bc[:, 0:TB],
               reads=[(pref + "bc", b)], writes=[(pref + "rstdB", b)], tag="ln_rstdB")
    for i in range(NCT):
        P.tt("v", out_tiles[i][:], x_tiles[i][:], muB[:], ALU.subtract,
             reads=[(xkey, i)] + [(pref + "muB", b) for b in range(NTB)],
             writes=[(okey, i)], tag="ln_sub")
        P.tt("v", out_tiles[i][:], out_tiles[i][:], rstdB[:], ALU.mult,
             reads=[(okey, i)] + [(pref + "rstdB", b) for b in range(NTB)],
             writes=[(okey, i)], tag="ln_mul")
        if gamma is not None:
            P.ts("v", out_tiles[i][:], out_tiles[i][:], gamma[i], beta[i],
                 ALU.mult, ALU.add,
                 reads=[(okey, i), gkey], writes=[(okey, i)], tag="ln_gb")


def alloc_ln_scratch(ctx, pref):
    ctx.sb(pref + "sq", [128, T], F32)
    ctx.sb(pref + "mu", [1, T], F32)
    ctx.sb(pref + "rstd", [1, T], F32)
    ctx.sb(pref + "t1", [1, T], F32)
    ctx.sb(pref + "muB", [128, T], F32)
    ctx.sb(pref + "rstdB", [128, T], F32)
    ctx.ps(pref + "ps0", [128, 512], F32)
    ctx.ps(pref + "ps1", [128, 512], F32)
    ctx.ps(pref + "bc", [128, 512], F32)


def stage_ln(ctx, src_name, dst_name, pref="ln_"):
    ctx.need_consts(["ones_col", "ones_row"])
    src = ctx.dram(src_name, [C, T], F32).ap()
    dst = ctx.dram(dst_name, [C, T], F32).ap()
    xt = [ctx.sb(f"{pref}x{i}", [128, T], F32) for i in range(NCT)]
    ot = [ctx.sb(f"{pref}o{i}", [128, T], F32) for i in range(NCT)]
    alloc_ln_scratch(ctx, pref)
    emit_load_act(ctx, xt, src, pref + "x", hname=src_name)
    emit_layernorm(ctx, xt, pref + "x", ot, pref + "o", pref)
    emit_store_act(ctx, ot, dst, pref + "o", dst_name)
    ctx.P.fence()
    for t_ in (pref + "bc", pref + "ps1", pref + "ps0"):
        ctx.free(t_)
    for t_ in (pref + "rstdB", pref + "muB", pref + "t1", pref + "rstd",
               pref + "mu", pref + "sq"):
        ctx.free(t_)
    for i in reversed(range(NCT)):
        ctx.free(f"{pref}o{i}")
    for i in reversed(range(NCT)):
        ctx.free(f"{pref}x{i}")




def wview3(w_ap, p=128):
    return w_ap.rearrange("(a p) o -> p a o", p=p)


def sbview3(tile, j):
    return tile[:].rearrange("p (a j) -> p a j", j=j)


def load_wcols(P, q, wbuf, w3, o0, ow, wkey, tag="wld"):
    a = w3.shape[1]
    view = wbuf[:, 0:a * ow].rearrange("p (a j) -> p a j", j=ow)
    P.dma(q, view, w3[:, :, o0:o0 + ow], writes=[wkey], tag=tag)



def stage_l0_attn(ctx):
    ctx.need_consts(["ones_col", "ones_row", "ones_row_b", "ident_f", "l0_perm",
                     "l0_sin", "l0_cos"])
    P, cs = ctx.P, ctx.consts
    xn_h = ctx.dram("xn_h", [C, T], F32).ap()
    w_inT = ctx.dram("w_inT", [C, 3 * C], F32).ap()
    w_outT = ctx.dram("w_outT", [C, C], F32).ap()
    gam_h = ctx.dram("l0_gamma", [128, NCT], F32).ap()
    bet_h = ctx.dram("l0_beta", [128, NCT], F32).ap()
    s1_h = ctx.dram("s1_h", [C, T], F32).ap()
    win3 = wview3(w_inT)

    attnT = [ctx.sb(f"a0_at{i}", [128, T], BF16) for i in range(NCT)]
    xnb = [ctx.sb(f"a0_xb{i}", [128, T], BF16) for i in range(NCT)]
    for i in range(NCT):
        P.dma("g", xnb[i][:], xn_h[128 * i:128 * (i + 1), :],
              reads=[("H", "xn_h")], writes=[("a0_xb", i)], tag="a0_ldxb")
    vww = [ctx.sb(f"a0_vw{k}", [128, NCT * 512], BF16) for k in range(2)]
    v_all = [ctx.sb(f"a0_v{t}", [128, C], BF16) for t in range(NTT)]
    pbig0 = ctx.ps("a0_p0", [128, T], F32)
    pbig1 = ctx.ps("a0_p1", [128, T], F32)
    PK0 = lambda b: ("P0", b)
    PK1 = lambda b: ("P1", b)
    P.psum_keys.update([PK0(b) for b in range(4)] + [PK1(b) for b in range(4)])
    qps = pbig1[:, 0:512]
    qpk = PK1(0)
    for ob in range(2):
        load_wcols(P, "g", vww[ob], win3, 2 * C + 512 * ob, 512, ("a0_vw", ob),
                   tag="a0_ldvw")
    for tt_ in range(NTT):
        tsl = slice(128 * tt_, 128 * (tt_ + 1))
        for ob in range(2):
            for ci in range(NCT):
                P.mm(qps, xnb[ci][:, tsl], vww[ob][:, 512 * ci:512 * (ci + 1)],
                     start=(ci == 0), stop=(ci == NCT - 1),
                     reads=[("a0_xb", ci), ("a0_vw", ob)], writes=[qpk],
                     tag="a0_mm_v")
            P.copy("v", v_all[tt_][:, 512 * ob:512 * (ob + 1)], qps,
                   reads=[qpk], writes=[("a0_v", tt_, ob)], tag="a0_ev_v")
    wq = [ctx.sb(f"a0_wq{k}", [128, NCT * 128], BF16) for k in range(2)]
    wk_ = [ctx.sb(f"a0_wk{k}", [128, NCT * 128], BF16) for k in range(2)]
    Q = ctx.sb("a0_Q", [128, T], BF16)
    Kt = ctx.sb("a0_K", [128, T], BF16)
    Qr = ctx.sb("a0_Qr", [128, T], BF16)
    Kr = ctx.sb("a0_Kr", [128, T], BF16)
    rsc1 = ctx.sb("a0_rs1", [128, TB], F32)
    rsc2 = ctx.sb("a0_rs2", [128, TB], BF16)
    den_scr = ctx.sb("a0_dscr", [128, T], BF16)
    den_asm = [ctx.sb(f"a0_da{h}", [128, 16], F32) for h in range(2)]
    rdT_sb = ctx.sb("a0_rdT", [16, 128], F32)
    rd_row = ctx.sb("a0_rdrow", [1, T], BF16)
    y_sb = ctx.sb("a0_ysb", [64, T], F32)
    Pt = ctx.sb("a0_Pt", [128, T], BF16)

    for hp in range(8):
        wb = hp % 2
        load_wcols(P, "g", wq[wb], win3, 128 * hp, 128, ("a0_wq", wb), tag="a0_ldwq")
        load_wcols(P, "g", wk_[wb], win3, C + 128 * hp, 128, ("a0_wk", wb),
                   tag="a0_ldwk")
        for tb in range(NTB):
            sl = slice(TB * tb, TB * (tb + 1))
            for dst, wsrc, wkey in ((Q, wq[wb], ("a0_wq", wb)),
                                    (Kt, wk_[wb], ("a0_wk", wb))):
                nm = "a0_Q" if dst is Q else "a0_K"
                for ci in range(NCT):
                    P.mm(qps, wsrc[:, 128 * ci:128 * (ci + 1)], xnb[ci][:, sl],
                         start=(ci == 0), stop=(ci == NCT - 1),
                         reads=[wkey, ("a0_xb", ci)], writes=[qpk], tag="a0_mm_qk")
                P.copy("v", dst[:, sl], qps, reads=[qpk], writes=[(nm, tb)],
                       tag="a0_ev_qk")
                rdst = Qr if dst is Q else Kr
                rnm = "a0_Qr" if dst is Q else "a0_Kr"
                P.mm(qps, cs["l0_perm"][:], dst[:, sl],
                     reads=["c_l0_perm", (nm, tb)], writes=[qpk], tag="a0_mm_perm")
                P.tt("v", rsc1[:], qps, cs["l0_sin"][:, sl], ALU.mult,
                     reads=[qpk, "c_l0_sin"], writes=[("a0_rs1",)], tag="a0_ropes")
                P.tt("v", rsc2[:], dst[:, sl], cs["l0_cos"][:, sl], ALU.mult,
                     reads=[(nm, tb), "c_l0_cos"], writes=[("a0_rs2",)],
                     tag="a0_ropec")
                P.tt("v", rdst[:, sl], rsc1[:], rsc2[:], ALU.add,
                     reads=[("a0_rs1",), ("a0_rs2",)], writes=[(rnm, tb)],
                     tag="a0_ropea")
        qkeys = [("a0_Q", tb) for tb in range(NTB)]
        kkeys = [("a0_K", tb) for tb in range(NTB)]
        qrkeys = [("a0_Qr", tb) for tb in range(NTB)]
        krkeys = [("a0_Kr", tb) for tb in range(NTB)]
        for qt in range(NTT):
            qsl = slice(128 * qt, 128 * (qt + 1))
            for kb in range(NTB):
                for h01 in range(2):
                    rb = 64 * h01
                    dps = pbig0 if h01 == 0 else pbig1
                    KK = PK0 if h01 == 0 else PK1
                    P.mm(dps[:, TB * kb:TB * (kb + 1)],
                         Q[rb:rb + 64, qsl], Kt[rb:rb + 64, TB * kb:TB * (kb + 1)],
                         tile_position=(rb, 0),
                         reads=qkeys + kkeys, writes=[KK(kb)], tag="a0_mm_den")
            for h01 in range(2):
                dps = pbig0 if h01 == 0 else pbig1
                KK = PK0 if h01 == 0 else PK1
                P.act(den_scr[:], dps[:], AF.Exp, scale=0.125,
                      accum_out=den_asm[h01][:, qt:qt + 1],
                      reads=[KK(kb) for kb in range(NTB)],
                      writes=[("a0_dscr",), ("a0_da", h01, qt)], tag="a0_expden")
        for h01 in range(2):
            rb = 64 * h01
            h = 2 * hp + h01
            at_tile = attnT[hp]
            P.recip(den_asm[h01][:], den_asm[h01][:],
                    reads=[("a0_da", h01, qt) for qt in range(NTT)],
                    writes=[("a0_da", h01)], tag="a0_rdrec")
            P.trans(pbig0[0:16, 0:128], den_asm[h01][:], cs["ident_f"][:],
                    reads=[("a0_da", h01), "c_ident_f"], writes=[PK0(0)],
                    tag="a0_rdtr")
            P.copy("v", rdT_sb[:], pbig0[0:16, 0:128], reads=[PK0(0)],
                   writes=[("a0_rdT",)], tag="a0_rdev")
            P.dma("g", rd_row[0:1, :], rdT_sb[:], reads=[("a0_rdT",)],
                  writes=[("a0_rdrow",)], tag="a0_rddma")
            for half in range(2):
                def emit_num(kt):
                    ksl = slice(128 * kt, 128 * (kt + 1))
                    pp = kt % 2
                    for qb in range(2):
                        qsl = slice(1024 * half + 512 * qb,
                                    1024 * half + 512 * (qb + 1))
                        P.mm(pbig0[:, 1024 * pp + 512 * qb:1024 * pp + 512 * (qb + 1)],
                             Kr[rb:rb + 64, ksl], Qr[rb:rb + 64, qsl],
                             tile_position=(rb, 0),
                             reads=krkeys + qrkeys, writes=[PK0(2 * pp + qb)],
                             tag="a0_mm_num")
                def emit_exp(kt):
                    pp = kt % 2
                    psl = slice(1024 * pp, 1024 * pp + 1024)
                    P.act(Pt[:, psl], pbig0[:, psl], AF.Exp, scale=0.125,
                          reads=[PK0(2 * pp), PK0(2 * pp + 1)],
                          writes=[("a0_Pt", pp)], tag="a0_expnum")
                def emit_av(kt):
                    pp = kt % 2
                    for qb in range(2):
                        P.mm(pbig1[rb:rb + 64, 512 * qb:512 * (qb + 1)],
                             v_all[kt][:, 64 * h:64 * h + 64],
                             Pt[:, 1024 * pp + 512 * qb:1024 * pp + 512 * (qb + 1)],
                             start=(kt == 0), stop=(kt == NTT - 1),
                             reads=[("a0_Pt", pp), ("a0_v", kt, 0), ("a0_v", kt, 1)],
                             writes=[PK1(qb)], tag="a0_mm_av")
                emit_num(0)
                for kt in range(NTT):
                    if kt + 1 < NTT:
                        emit_num(kt + 1)
                    emit_exp(kt)
                    emit_av(kt)
                P.copy("v", y_sb[:, 1024 * half:1024 * (half + 1)],
                       pbig1[rb:rb + 64, 0:1024],
                       reads=[PK1(0), PK1(1)], writes=[("a0_ysb", half)],
                       tag="a0_ev_y")
            for tb in range(NTB):
                sl = slice(TB * tb, TB * (tb + 1))
                bb = tb % 2
                P.mm(pbig1[0:64, 512 * bb:512 * bb + 512],
                     cs["ones_row_b"][0:1, 0:64], rd_row[0:1, sl],
                     reads=[("a0_rdrow",), "c_ones_row_b"], writes=[PK1(bb)],
                     tag="a0_mm_dbc")
                P.tt("v", at_tile[rb:rb + 64, sl], y_sb[:, sl],
                     pbig1[0:64, 512 * bb:512 * bb + 512], ALU.mult,
                     reads=[("a0_ysb", 0), ("a0_ysb", 1), PK1(bb)],
                     writes=[("a0_at", hp, h01, tb)], tag="a0_div")
    P.fence()
    for nm in (["a0_Pt", "a0_ysb", "a0_rdrow", "a0_rdT", "a0_da1", "a0_da0",
                "a0_dscr", "a0_rs2", "a0_rs1", "a0_Kr", "a0_Qr", "a0_K", "a0_Q"]
               + [f"a0_wk{k}" for k in (1, 0)] + [f"a0_wq{k}" for k in (1, 0)]
               + [f"a0_v{t}" for t in reversed(range(NTT))]
               + [f"a0_vw{k}" for k in (1, 0)]
               + [f"a0_xb{i}" for i in reversed(range(NCT))] + ["a0_p1"]):
        ctx.free(nm)
    wo = [ctx.sb(f"a0_wo{k}", [128, NCT * 128], BF16) for k in range(NCT)]
    h_t = [ctx.sb(f"a0_h{i}", [128, T], F32) for i in range(NCT)]
    xsl = [ctx.sb(f"a0_xsl{k}", [128, TB], F32) for k in range(2)]
    gam = ctx.sb("a0_gam", [128, NCT], F32)
    bet = ctx.sb("a0_bet", [128, NCT], F32)
    P.dma("y", gam[:], gam_h, writes=["c_gb"], tag="a0_ldg")
    P.dma("y", bet[:], bet_h, writes=["c_gb"], tag="a0_ldb")
    woT3 = wview3(w_outT)
    for o in range(NCT):
        load_wcols(P, "g", wo[o], woT3, 128 * o, 128, ("a0_wo", o), tag="a0_ldwo")
    for o in range(NCT):
        for tb in range(NTB):
            sl = slice(TB * tb, TB * (tb + 1))
            pk = tb % 2
            P.dma("y", xsl[pk][:], xn_h[128 * o:128 * (o + 1), sl],
                  reads=[("H", "xn_h")], writes=[("a0_xsl", pk)], tag="a0_ldxn")
            P.mm(pbig0[:, 0:512], wo[o][:, 0:128], attnT[0][:, sl],
                 start=True, stop=False,
                 reads=[("a0_wo", o)] + [("a0_at", 0, h01, tb) for h01 in range(2)],
                 writes=[PK0(0)], tag="a0_mm_wo")
            for ci in range(1, NCT):
                P.mm(pbig0[:, 0:512], wo[o][:, 128 * ci:128 * (ci + 1)],
                     attnT[ci][:, sl], start=False, stop=(ci == NCT - 1),
                     reads=[("a0_wo", o)] + [("a0_at", ci, h01, tb) for h01 in range(2)],
                     writes=[PK0(0)], tag="a0_mm_wo")
            P.tt("v", h_t[o][:, sl], pbig0[:, 0:512], xsl[pk][:], ALU.add,
                 reads=[PK0(0), ("a0_xsl", pk)], writes=[("a0_h", o)],
                 tag="a0_hadd")
    alloc_ln_scratch(ctx, "a0ln_")
    emit_layernorm(ctx, h_t, "a0_h", h_t, "a0_h", "a0ln_",
                   gamma=[gam[:, i:i + 1] for i in range(NCT)],
                   beta=[bet[:, i:i + 1] for i in range(NCT)])
    emit_store_act(ctx, h_t, s1_h, "a0_h", "s1_h")
    P.fence()
    for t_ in ("a0ln_bc", "a0ln_ps1", "a0ln_ps0", "a0_p0"):
        ctx.free(t_)
    for t_ in ("a0ln_rstdB", "a0ln_muB", "a0ln_t1", "a0ln_rstd", "a0ln_mu",
               "a0ln_sq", "a0_bet", "a0_gam", "a0_xsl1", "a0_xsl0"):
        ctx.free(t_)
    for i in reversed(range(NCT)):
        ctx.free(f"a0_h{i}")
    for k in reversed(range(NCT)):
        ctx.free(f"a0_wo{k}")
    for i in reversed(range(NCT)):
        ctx.free(f"a0_at{i}")



def stage_l0_ffn_combine(ctx):
    P = ctx.P
    xn_h = ctx.dram("xn_h", [C, T], F32).ap()
    x_h = ctx.dram("x_in", [C, T], F32).ap()
    s1_h = ctx.dram("s1_h", [C, T], F32).ap()
    x1_h = ctx.dram("x1_h", [C, T], F32).ap()
    wgT = ctx.dram("wgT", [C, 3072], I32).ap()
    wuT = ctx.dram("wuT", [C, 3072], I32).ap()
    wdT = ctx.dram("wdT", [3072, C], I32).ap()
    HID = 3072
    NJ = HID // 128

    xb = [ctx.sb(f"ff_xb{i}", [128, T], BF16) for i in range(NCT)]
    for i in range(NCT):
        P.dma("g", xb[i][:], xn_h[128 * i:128 * (i + 1), :],
              reads=[("H", "xn_h")], writes=[("ff_xb", i)], tag="ff_ldxb")
    wg_b = [ctx.sb(f"ff_wg{k}", [128, NCT * 128], BF16) for k in range(4)]
    wu_b = [ctx.sb(f"ff_wu{k}", [128, NCT * 128], BF16) for k in range(4)]
    wd_b = [ctx.sb(f"ff_wd{j}", [128, C], BF16) for j in range(NJ)]
    wg3, wu3 = wview3(wgT), wview3(wuT)
    for j in range(NJ):
        P.dma("g", wd_b[j][:], wdT[128 * j:128 * (j + 1), :], writes=[("ff_wd", j)],
              tag="ff_ldwd")
    hb = [ctx.sb(f"ff_hb{j}", [128, TB], BF16) for j in range(NJ)]
    g_ps = ctx.ps("ff_gps", [128, TB], F32)
    u_ps = ctx.ps("ff_ups", [128, TB], F32)
    o_ps = [ctx.ps(f"ff_ops{k}", [128, TB], F32) for k in range(2)]
    hs = ctx.sb("ff_hs", [128, TB], F32)
    gt = ctx.sb("ff_gt", [128, TB], F32)
    xsl = [ctx.sb(f"ff_xsl{k}", [128, TB], F32) for k in range(2)]
    ssl = [ctx.sb(f"ff_ssl{k}", [128, TB], F32) for k in range(2)]
    o1 = [ctx.sb(f"ff_o1{k}", [128, TB], F32) for k in range(2)]

    D = 1.0 / 127.0
    for tb in range(NTB):
        sl = slice(TB * tb, TB * (tb + 1))
        for j in range(NJ):
            wk = j % 4
            load_wcols(P, "g", wg_b[wk], wg3, 128 * j, 128, ("ff_wg", wk))
            load_wcols(P, "g", wu_b[wk], wu3, 128 * j, 128, ("ff_wu", wk))
            for ci in range(NCT):
                P.mm(g_ps[:], wg_b[wk][:, 128 * ci:128 * (ci + 1)], xb[ci][:, sl],
                     start=(ci == 0), stop=(ci == NCT - 1),
                     reads=[("ff_wg", wk), ("ff_xb", ci)], writes=[("ff_gps",)],
                     tag="ff_mm_g")
            for ci in range(NCT):
                P.mm(u_ps[:], wu_b[wk][:, 128 * ci:128 * (ci + 1)], xb[ci][:, sl],
                     start=(ci == 0), stop=(ci == NCT - 1),
                     reads=[("ff_wu", wk), ("ff_xb", ci)], writes=[("ff_ups",)],
                     tag="ff_mm_u")
            P.act(hs[:], g_ps[:], AF.Silu, scale=D,
                  reads=[("ff_gps",)], writes=[("ff_hs",)], tag="ff_silu")
            P.tt("v", hb[j][:], hs[:], u_ps[:], ALU.mult,
                 reads=[("ff_hs",), ("ff_ups",)], writes=[("ff_hb", j)],
                 tag="ff_hmul")
        for o in range(NCT):
            pk = o % 2
            P.dma("y", xsl[pk][:], x_h[128 * o:128 * (o + 1), sl],
                  reads=[("H", "x_in")], writes=[("ff_xsl", pk)], tag="ff_ldx")
            P.dma("y", ssl[pk][:], s1_h[128 * o:128 * (o + 1), sl],
                  reads=[("H", "s1_h")], writes=[("ff_ssl", pk)], tag="ff_lds")
            for j in range(NJ):
                P.mm(o_ps[pk][:], wd_b[j][:, 128 * o:128 * (o + 1)], hb[j][:],
                     start=(j == 0), stop=(j == NJ - 1),
                     reads=[("ff_wd", j), ("ff_hb", j)], writes=[("ff_ops", pk)],
                     tag="ff_mm_o")
            P.act(gt[:], o_ps[pk][:], AF.Silu, scale=D * D,
                  reads=[("ff_ops", pk)], writes=[("ff_gt",)], tag="ff_silu2")
            P.tt("v", o1[pk][:], gt[:], ssl[pk][:], ALU.mult,
                 reads=[("ff_gt",), ("ff_ssl", pk)], writes=[("ff_o1", pk)],
                 tag="ff_smul")
            P.tt("v", o1[pk][:], o1[pk][:], xsl[pk][:], ALU.add,
                 reads=[("ff_o1", pk), ("ff_xsl", pk)], writes=[("ff_o1", pk)],
                 tag="ff_xadd")
            P.dma("y", x1_h[128 * o:128 * (o + 1), sl], o1[pk][:],
                  reads=[("ff_o1", pk)], writes=[("H", "x1_h")], tag="ff_st")
    P.fence()
    for t_ in ("ff_ops1", "ff_ops0", "ff_ups", "ff_gps"):
        ctx.free(t_)
    for t_ in (["ff_o11", "ff_o10", "ff_ssl1", "ff_ssl0", "ff_xsl1", "ff_xsl0",
                "ff_gt", "ff_hs"]
               + [f"ff_hb{j}" for j in reversed(range(NJ))]
               + [f"ff_wd{j}" for j in reversed(range(NJ))]
               + [f"ff_wu{k}" for k in reversed(range(4))]
               + [f"ff_wg{k}" for k in reversed(range(4))]
               + [f"ff_xb{i}" for i in reversed(range(NCT))]):
        ctx.free(t_)



def stage_rwkv(ctx):
    P, cs = ctx.P, ctx.consts
    xh = ctx.dram("xn2_h", [C, T], F32).ap()
    s2_h = ctx.dram("s2_h", [C, T], F32).ap()
    Wk_h = wview3(ctx.dram("WkT", [C, C], F32).ap())
    Wv_h = wview3(ctx.dram("WvT", [C, C], F32).ap())
    Wr_h = wview3(ctx.dram("WrT", [C, C], F32).ap())
    Wo_h = wview3(ctx.dram("WoT", [C, C], F32).ap())
    par = ctx.sb("rk_par", [128, 5 * NCT], F32)
    P.dma("y", par[:], ctx.dram("rwkv_par", [128, 5 * NCT], F32).ap(),
          writes=[("rk_par",)], tag="rk_ldpar")
    TD = lambda i: par[:, i:i + 1]
    TF = lambda i: par[:, NCT + i:NCT + i + 1]
    TM = lambda m, i: par[:, (2 + m) * NCT + i:(2 + m) * NCT + i + 1]

    ryb = [ctx.sb(f"rk_ry{i}", [128, T], BF16) for i in range(NCT)]
    xkb = [ctx.sb(f"rk_xk{i}", [128, T], BF16) for i in range(NCT)]
    xvb = [ctx.sb(f"rk_xv{i}", [128, T], BF16) for i in range(NCT)]
    xrb = [ctx.sb(f"rk_xr{i}", [128, T], BF16) for i in range(NCT)]
    xt = [ctx.sb(f"rk_x{i}", [128, T], F32) for i in range(NCT)]
    dscr = ctx.sb("rk_d", [128, T], F32)
    for i in range(NCT):
        P.dma("y", xt[i][:], xh[128 * i:128 * (i + 1), :],
              reads=[("H", "xn2_h")], writes=[("rk_x", i)], tag="rk_ldx")
    for i in range(NCT):
        P.tt("v", dscr[:, 1:T], xt[i][:, 1:T], xt[i][:, 0:T - 1], ALU.subtract,
             reads=[("rk_x", i)], writes=[("rk_d",)], tag="rk_dsub")
        P.copy("v", dscr[:, 0:1], xt[i][:, 0:1], reads=[("rk_x", i)],
               writes=[("rk_d",)], tag="rk_dcol0")
        for m, dst, nm in ((0, xkb, "rk_xk"), (1, xvb, "rk_xv"), (2, xrb, "rk_xr")):
            P.stt("v", dst[i][:, 1:T], dscr[:, 1:T], TM(m, i), xt[i][:, 0:T - 1],
                  ALU.mult, ALU.add,
                  reads=[("rk_d",), ("rk_x", i), ("rk_par",)], writes=[(nm, i)],
                  tag="rk_mix")
            P.ts("v", dst[i][:, 0:1], xt[i][:, 0:1], TM(m, i), None, ALU.mult,
                 reads=[("rk_x", i), ("rk_par",)], writes=[(nm, i)], tag="rk_mix0")
    P.fence()
    ctx.free("rk_d")
    for i in reversed(range(NCT)):
        ctx.free(f"rk_x{i}")
    kf = ctx.sb("rk_k", [128, T], F32)
    ek = ctx.sb("rk_ek", [128, T], F32)
    ekv = ctx.sb("rk_ekv", [128, T], F32)
    rb = ctx.sb("rk_rb", [128, T], BF16)
    As = ctx.sb("rk_A", [128, T + 1], F32)
    Bs = ctx.sb("rk_B", [128, T + 1], F32)
    Ps = ctx.sb("rk_P", [128, T + 1], F32)
    num = ctx.sb("rk_num", [128, T], F32)
    den = ctx.sb("rk_den", [128, T], F32)
    smc = ctx.sb("rk_smc", [128, 8], F32)
    wv = [ctx.sb(f"rk_wv{k}", [128, NCT * 128], BF16) for k in range(4)]
    pk = ctx.ps("rk_pk", [128, TB], F32)
    pv2 = ctx.ps("rk_pv", [128, TB], F32)
    pr = ctx.ps("rk_pr", [128, TB], F32)
    P.psum_keys.update([("rk_pk",), ("rk_pv",), ("rk_pr",)])
    P.memset("g", smc[:, 3:4], 0.0, writes=[("rk_smc", 3)], tag="rk_z")
    P.memset("g", As[:, 0:1], 0.0, writes=[("rk_A", "i")], tag="rk_a0")
    P.memset("g", Bs[:, 0:1], 0.0, writes=[("rk_B", "i")], tag="rk_b0")
    P.memset("g", Ps[:, 0:1], -1e30, writes=[("rk_P", "i")], tag="rk_p0")
    for i in range(NCT):
        P.act(smc[:, 0:1], TD(i), AF.Exp, scale=1.0, reads=[("rk_par",)],
              writes=[("rk_smc", 0)], tag="rk_w")
        P.ts("v", smc[:, 0:1], smc[:, 0:1], -1.0, None, ALU.mult,
             reads=[("rk_smc", 0)], writes=[("rk_smc", 0)], tag="rk_wneg")
        P.act(smc[:, 1:2], smc[:, 0:1], AF.Exp, reads=[("rk_smc", 0)],
              writes=[("rk_smc", 1)], tag="rk_ew")
        P.act(smc[:, 2:3], TF(i), AF.Exp, reads=[("rk_par",)],
              writes=[("rk_smc", 2)], tag="rk_eu")
        for m, (W3, ps, pko) in enumerate(((Wk_h, pk, ("rk_pk",)),
                                           (Wv_h, pv2, ("rk_pv",)),
                                           (Wr_h, pr, ("rk_pr",)))):
            wb = wv[m]
            load_wcols(P, "g", wb, W3, 128 * i, 128, ("rk_wv", m), tag="rk_ldw")
            src = (xkb, xvb, xrb)[m]
            snm = ("rk_xk", "rk_xv", "rk_xr")[m]
            for tb in range(NTB):
                sl = slice(TB * tb, TB * (tb + 1))
                for cj in range(NCT):
                    P.mm(ps[:], wb[:, 128 * cj:128 * (cj + 1)], src[cj][:, sl],
                         start=(cj == 0), stop=(cj == NCT - 1),
                         reads=[("rk_wv", m), (snm, cj)], writes=[pko],
                         tag="rk_mm")
                if m == 0:
                    P.copy("v", kf[:, sl], ps[:], reads=[pko], writes=[("rk_k",)],
                           tag="rk_evk")
                elif m == 1:
                    P.act(ek[:, sl], kf[:, sl], AF.Exp, reads=[("rk_k",)],
                          writes=[("rk_ek",)], tag="rk_expk")
                    P.tt("v", ekv[:, sl], ek[:, sl], ps[:], ALU.mult,
                         reads=[("rk_ek",), pko], writes=[("rk_ekv",)],
                         tag="rk_ekv")
                else:
                    P.act(rb[:, sl], ps[:], AF.Sigmoid, reads=[pko],
                          writes=[("rk_rb",)], tag="rk_sig")
        kkeys = [("rk_k",)]
        ekkeys = [("rk_ek",)]
        ekvkeys = [("rk_ekv",)]
        ewB = bass.AP(smc, 1, [[8, 128], [0, T]])
        wB = bass.AP(smc, 0, [[8, 128], [0, T]])
        P.scan(As[:, 1:T + 1], ewB, ekv[:], smc[:, 3:4], ALU.mult, ALU.add,
               reads=ekvkeys + [("rk_smc", 1), ("rk_smc", 3)], writes=[("rk_A",)],
               tag="rk_scanA")
        P.scan(Bs[:, 1:T + 1], ewB, ek[:], smc[:, 3:4], ALU.mult, ALU.add,
               reads=ekkeys + [("rk_smc", 1), ("rk_smc", 3)], writes=[("rk_B",)],
               tag="rk_scanB")
        P.scan(Ps[:, 1:T + 1], wB, kf[:], Ps[:, 0:1], ALU.add, ALU.max,
               reads=kkeys + [("rk_smc", 0), ("rk_P", "i")], writes=[("rk_P",)],
               tag="rk_scanP")
        P.stt("v", num[:], As[:, 0:T], smc[:, 2:3], ekv[:], ALU.mult, ALU.add,
              reads=[("rk_A",), ("rk_A", "i"), ("rk_smc", 2)] + ekvkeys,
              writes=[("rk_num",)], tag="rk_num")
        P.stt("v", den[:], Bs[:, 0:T], smc[:, 2:3], ek[:], ALU.mult, ALU.add,
              reads=[("rk_B",), ("rk_B", "i"), ("rk_smc", 2)] + ekkeys,
              writes=[("rk_den",)], tag="rk_den")
        P.stt("v", ek[:], Ps[:, 0:T], TF(i), kf[:], ALU.add, ALU.max,
              reads=[("rk_P",), ("rk_P", "i"), ("rk_par",)] + kkeys,
              writes=[("rk_ek2",)], tag="rk_eparg")
        P.act(ek[:], ek[:], AF.Exp, bias=ctx.scal(math.log(1e-6)),
              reads=[("rk_ek2",), "c_scal"], writes=[("rk_ek2",)], tag="rk_eps")
        P.tt("v", den[:], den[:], ek[:], ALU.add,
             reads=[("rk_den",), ("rk_ek2",)], writes=[("rk_den",)], tag="rk_dene")
        P.recip(den[:], den[:], reads=[("rk_den",)], writes=[("rk_den",)],
                tag="rk_rec")
        P.tt("v", num[:], num[:], den[:], ALU.mult,
             reads=[("rk_num",), ("rk_den",)], writes=[("rk_num",)], tag="rk_y")
        P.tt("v", ryb[i][:], num[:], rb[:], ALU.mult,
             reads=[("rk_num",), ("rk_rb",)],
             writes=[("rk_ry", i)], tag="rk_ry")
    out_s = ctx.sb("rk_out", [128, TB], F32)
    for o in range(NCT):
        wb = wv[3]
        load_wcols(P, "g", wb, Wo_h, 128 * o, 128, ("rk_wv", 3), tag="rk_ldwo")
        for tb in range(NTB):
            sl = slice(TB * tb, TB * (tb + 1))
            for cj in range(NCT):
                P.mm(pk[:], wb[:, 128 * cj:128 * (cj + 1)], ryb[cj][:, sl],
                     start=(cj == 0), stop=(cj == NCT - 1),
                     reads=[("rk_wv", 3), ("rk_ry", cj)], writes=[("rk_pk",)],
                     tag="rk_mm_o")
            P.copy("v", out_s[:], pk[:], reads=[("rk_pk",)], writes=[("rk_out",)],
                   tag="rk_evo")
            P.dma("y", s2_h[128 * o:128 * (o + 1), sl], out_s[:],
                  reads=[("rk_out",)], writes=[("H", "s2_h")], tag="rk_st")
    P.fence()
    ctx.free("rk_out")
    for t_ in ("rk_pr", "rk_pv", "rk_pk"):
        ctx.free(t_)
    for t_ in ([f"rk_wv{k}" for k in reversed(range(4))]
               + ["rk_smc", "rk_den", "rk_num", "rk_P", "rk_B", "rk_A",
                  "rk_rb", "rk_ekv", "rk_ek", "rk_k"]
               + [f"rk_xr{i}" for i in reversed(range(NCT))]
               + [f"rk_xv{i}" for i in reversed(range(NCT))]
               + [f"rk_xk{i}" for i in reversed(range(NCT))]
               + [f"rk_ry{i}" for i in reversed(range(NCT))]
               + ["rk_par"]):
        ctx.free(t_)



INNER_JT = [128] * 21 + [42]
DIM_INNER = 2730

INNER_DBG = 99

def emit_inner_attn(ctx, li, src, mid, srcn="src", midn="mid"):
    P, cs = ctx.P, ctx.consts
    nm = f"i{li}a_"
    Wqkv = ctx.dram("L1_Wqkv", [6, C, 3 * C], F32).ap()
    Wo_h = ctx.dram("L1_Wo", [6, C, C], F32).ap()
    qs_h = ctx.dram("L1_qs", [6, 128, 1], F32).ap()
    ks_h = ctx.dram("L1_ks", [6, 128, 1], F32).ap()
    w3 = Wqkv[li].rearrange("(a p) o -> p a o", p=128)
    wo3 = Wo_h[li].rearrange("(a p) o -> p a o", p=128)
    NB = T // 256

    attnT = [ctx.sb(nm + f"at{i}", [128, T], BF16) for i in range(NCT)]
    rsq = ctx.sb(nm + "rsq", [16, T], F32)
    rsk = ctx.sb(nm + "rsk", [16, T], F32)
    qscol = ctx.sb(nm + "qsc", [128, 4], F32)
    Pts = ctx.sb(nm + "Pts", [128, 768], BF16)
    mxA = ctx.sb(nm + "mxA", [128, 32], F32)
    mxT = ctx.sb(nm + "mxT", [16, 128], F32)
    mxrow = [ctx.sb(nm + f"mxr{k}", [1, T], BF16) for k in range(2)]
    ysb = ctx.sb(nm + "ysb", [64, 256], F32)
    rdr = ctx.sb(nm + "rdr", [1, 256], F32)
    sqs = [ctx.sb(nm + f"sq{k}", [128, TB], BF16) for k in range(2)]
    rps = [ctx.sb(nm + f"rp{k}", [128, TB], BF16) for k in range(2)]
    wbq = [ctx.sb(nm + f"wbq{k}", [128, NCT * 128], BF16) for k in range(2)]
    wbv = ctx.sb(nm + "wbv", [128, NCT * 256], BF16)
    qb = [ctx.sb(nm + f"q{i}", [128, T], BF16) for i in range(4)]
    kb = [ctx.sb(nm + f"k{i}", [128, T], BF16) for i in range(4)]
    den_asm = ctx.sb(nm + "dna", [128, 32], F32)
    dnrow = [ctx.sb(nm + f"dnr{k}", [1, T], BF16) for k in range(2)]
    vv = [ctx.sb(nm + f"v{t}", [128, 512], BF16) for t in range(NTT)]
    if li == 0:
        xb = [ctx.sb(nm + f"xb{i}", [128, T], BF16) for i in range(NCT)]
    else:
        xb = [ctx.tiles[nm + f"xb{i}"] for i in range(NCT)]
    pq = ctx.ps(nm + "pq", [128, 512], F32)
    pc = ctx.ps(nm + "pc", [128, 512], F32)
    pa = ctx.ps(nm + "pa", [128, 512], F32)
    pb = ctx.ps(nm + "pb", [128, 512], F32)
    pa2 = ctx.ps(nm + "pa2", [128, 512], F32)
    pb2 = ctx.ps(nm + "pb2", [128, 512], F32)
    py = ctx.ps(nm + "py", [128, 512], F32)
    pe = ctx.ps(nm + "pe", [128, 512], F32)
    KQ, KC, KA, KB, KA2, KB2, KY, KE = (nm + "PQ",), (nm + "PC",), (nm + "PA",), \
        (nm + "PB",), (nm + "PA2",), (nm + "PB2",), (nm + "PY",), (nm + "PE",)
    P.psum_keys.update([KQ, KC, KA, KB, KA2, KB2, KY, KE])

    P.dma("y", qscol[:, 0:1], qs_h[li], writes=[(nm + "qsc",)], tag=nm + "ldqs")
    P.dma("y", qscol[:, 1:2], ks_h[li], writes=[(nm + "qsc",)], tag=nm + "ldks")
    P.mm(pq[:, 0:2], cs["in_perm_f"][:], qscol[:, 0:2],
         reads=[(nm + "qsc",), "c_in_perm_f"], writes=[(nm + "PQ",)], tag=nm + "qsr")
    P.act(qscol[:, 2:4], pq[:, 0:2], AF.Abs, reads=[(nm + "PQ",)],
          writes=[(nm + "qsc",)], tag=nm + "qsab")
    P.recip(pq[:, 2:4], qscol[:, 0:2], reads=[(nm + "qsc",), (nm + "PQ",)],
            writes=[(nm + "PQ",)], tag=nm + "qsrc")
    P.tt("v", qscol[:, 2:4], qscol[:, 2:4], pq[:, 2:4], ALU.mult,
         reads=[(nm + "qsc",), (nm + "PQ",)], writes=[(nm + "qsc",)],
         tag=nm + "qsrr")
    for i in range(NCT):
        if li == 0:
            if i < 4:
                P.dma("g", xb[i][:], src[128 * i:128 * (i + 1), :],
                      reads=[("H", srcn)], writes=[(nm + "xb", i)], tag=nm + "xbc")
            else:
                P.memset("g", xb[i][:, 0:1], 0.0, writes=[(nm + "xb", i)],
                         tag=nm + "xz")
                P.dma("g", xb[i][:, 1:T], src[128 * i:128 * (i + 1), 0:T - 1],
                      reads=[("H", srcn)], writes=[(nm + "xb", i)], tag=nm + "xbs")
        elif i >= 4:
            P.memset("g", xb[i][:, 0:1], 0.0, writes=[(nm + "xb", i)],
                     tag=nm + "xz")

    for hg in range(2 if INNER_DBG >= 1 else 0):
        def qk_produce(dst, dkey, obase, scol, rs_t, rk):
            del scol
            for otl in range(4):
                wk = otl % 2
                load_wcols(P, "g", wbq[wk], w3, obase + 128 * otl, 128,
                           (nm + "wbq", wk), tag=nm + "ldw")
                for tb in range(NTB):
                    sl = slice(TB * tb, TB * (tb + 1))
                    ps, pk_ = (pq, KQ) if tb % 2 == 0 else (pc, KC)
                    for ci in range(NCT):
                        P.mm(ps[:], wbq[wk][:, 128 * ci:128 * (ci + 1)],
                             xb[ci][:, sl],
                             start=(ci == 0), stop=(ci == NCT - 1),
                             reads=[(nm + "wbq", wk), (nm + "xb", ci)],
                             writes=[pk_], tag=nm + "mmqk")
                    P.copy("v", dst[otl][:, sl], ps[:],
                           reads=[pk_], writes=[(dkey, otl, tb)],
                           tag=nm + "evqk")
            for tb in range(NTB):
                sl = slice(TB * tb, TB * (tb + 1))
                nps, nk = (pa, KA) if tb % 2 == 0 else (pb, KB)
                for otl in range(4):
                    sq = sqs[otl % 2]
                    P.tt("v", sq[:], dst[otl][:, sl], dst[otl][:, sl], ALU.mult,
                         reads=[(dkey, otl, tb)], writes=[(nm + "sq", otl % 2)],
                         tag=nm + "sqev")
                    P.mm(nps[0:16, 0:TB], cs["hmask"][:, 16 * otl:16 * (otl + 1)],
                         sq[:], start=(otl == 0), stop=(otl == 3),
                         reads=[(nm + "sq", otl % 2), "c_hmask"], writes=[nk],
                         tag=nm + "mmsq")
                P.copy("v", rs_t[:, sl], nps[0:16, 0:TB],
                       reads=[nk], writes=[rk], tag=nm + "nacc")
            P.act(rs_t[:], rs_t[:], AF.Sqrt, reads=[rk], writes=[rk],
                  tag=nm + "nsq")
            P.ts("v", rs_t[:], rs_t[:], 1e-12, None, ALU.max, reads=[rk],
                 writes=[rk], tag=nm + "nmx")
            P.recip(rs_t[:], rs_t[:], reads=[rk], writes=[rk], tag=nm + "nrc")
        qk_produce(qb, nm + "q", 512 * hg, qscol[:, 0:1], rsq, (nm + "rsq",))
        qk_produce(kb, nm + "k", C + 512 * hg, qscol[:, 1:2], rsk, (nm + "rsk",))
        if INNER_DBG < 2:
            continue
        if INNER_DBG < 1.7:
            continue
        load_wcols(P, "g", wbv, w3, 2 * C + 512 * hg, 256, (nm + "wbv",),
                   tag=nm + "ldwv1")
        for vh in range(2):
            if vh == 1:
                load_wcols(P, "g", wbv, w3, 2 * C + 512 * hg + 256, 256,
                           (nm + "wbv",), tag=nm + "ldwv2")
            for t_ in range(NTT):
                tsl = slice(128 * t_, 128 * (t_ + 1))
                ps, pk_ = (pq, KQ) if t_ % 2 == 0 else (pc, KC)
                for ci in range(NCT):
                    P.mm(ps[:, 0:256], xb[ci][:, tsl],
                         wbv[:, 256 * ci:256 * (ci + 1)],
                         start=(ci == 0), stop=(ci == NCT - 1),
                         reads=[(nm + "wbv",), (nm + "xb", ci)], writes=[pk_],
                         tag=nm + "mmv")
                P.copy("v", vv[t_][:, 256 * vh:256 * (vh + 1)], ps[:, 0:256],
                       reads=[pk_], writes=[(nm + "v", t_)], tag=nm + "vev")
        if INNER_DBG < 3:
            continue
        for dst, dkey, cosn, sinn, rs_t, rk in (
                (qb, nm + "q", "cosq", "sinq", rsq, (nm + "rsq",)),
                (kb, nm + "k", "cosk", "sink", rsk, (nm + "rsk",))):
            for otl in range(4):
                for tb in range(NTB):
                    sl = slice(TB * tb, TB * (tb + 1))
                    bps, bk = (pa, KA) if tb % 2 == 0 else (pb, KB)
                    P.mm(bps[:], cs["bcmask"][:, 128 * otl:128 * (otl + 1)],
                         rs_t[0:16, sl], start=True, stop=True,
                         reads=[rk, "c_bcmask"], writes=[bk], tag=nm + "mmbc")
                    sps, sk_ = (pa2, KA2) if tb % 2 == 0 else (pb2, KB2)
                    P.mm(sps[:], cs["in_perm"][:], dst[otl][:, sl],
                         start=True, stop=True,
                         reads=[(dkey, otl, tb), "c_in_perm"], writes=[sk_],
                         tag=nm + "mmsh")
                    scq = qscol[:, 0:1] if dst is qb else qscol[:, 1:2]
                    scr = qscol[:, 2:3] if dst is qb else qscol[:, 3:4]
                    t1 = sqs[tb % 2]
                    P.stt("v", t1[:], sps[:], scr, cs[sinn][:, sl], ALU.mult,
                          ALU.mult, reads=[sk_, "c_" + sinn, (nm + "qsc",)],
                          writes=[(nm + "sq", tb % 2)], tag=nm + "rsin")
                    t2 = rps[tb % 2]
                    P.tt("v", t2[:], dst[otl][:, sl], cs[cosn][:, sl], ALU.mult,
                         reads=[(dkey, otl, tb), "c_" + cosn],
                         writes=[(nm + "rp", tb % 2)], tag=nm + "rcos")
                    P.tt("v", t1[:], t1[:], t2[:], ALU.add,
                         reads=[(nm + "sq", tb % 2), (nm + "rp", tb % 2)],
                         writes=[(nm + "sq", tb % 2)], tag=nm + "radd")
                    P.stt("v", dst[otl][:, sl], t1[:], scq, bps[:], ALU.mult,
                          ALU.mult, reads=[(nm + "sq", tb % 2), bk, (nm + "qsc",)],
                          writes=[(dkey, otl, tb)], tag=nm + "rrs")
        qkeys = [(nm + "q", o, tb) for o in range(4) for tb in range(NTB)]
        kkeys = [(nm + "k", o, tb) for o in range(4) for tb in range(NTB)]
        if INNER_DBG < 4:
            continue
        for hp in range(4):
            for qt in range(NTT):
                k0 = max(0, 128 * qt - 256)
                kw = 128 * qt + 128 - k0
                msk = {128: "mq0", 256: "mq1"}.get(kw, "mband")
                for h01 in range(2):
                    rb = 64 * h01
                    mps, mk = ((pa, KA) if h01 == 0 else (pb, KB)) if qt % 2 == 0 \
                        else ((pa2, KA2) if h01 == 0 else (pb2, KB2))
                    P.mm(mps[:, 0:kw], qb[hp][rb:rb + 64, 128 * qt:128 * (qt + 1)],
                         kb[hp][rb:rb + 64, k0:k0 + kw], tile_position=(rb, 0),
                         reads=qkeys + kkeys, writes=[mk], tag=nm + "mmmx")
                for h01 in range(2):
                    mps, mk = ((pa, KA) if h01 == 0 else (pb, KB)) if qt % 2 == 0 \
                        else ((pa2, KA2) if h01 == 0 else (pb2, KB2))
                    P.tt("v", mps[:, 0:kw], mps[:, 0:kw], cs[msk][:, 0:kw],
                         ALU.add, reads=[mk, "c_" + msk], writes=[mk],
                         tag=nm + "mxmask")
                    P.ts("v", sqs[h01][:, 0:kw], mps[:, 0:kw], -8.0, 3e38,
                         ALU.mult, op1=ALU.min,
                         accum_out=mxA[:, 16 * h01 + qt:16 * h01 + qt + 1],
                         reads=[mk], writes=[(nm + "sq", h01), (nm + "mxA", h01)],
                         tag=nm + "mxred")
                    P.act(rps[h01][:, 0:kw], mps[:, 0:kw], AF.Exp, scale=8.0,
                          bias=mxA[:, 16 * h01 + qt:16 * h01 + qt + 1],
                          accum_out=den_asm[:, 16 * h01 + qt:16 * h01 + qt + 1],
                          reads=[mk, (nm + "mxA", h01)],
                          writes=[(nm + "rp", h01), (nm + "dna",)],
                          tag=nm + "dexp")
            for h01 in range(2):
                P.recip(den_asm[:, 16 * h01:16 * (h01 + 1)],
                        den_asm[:, 16 * h01:16 * (h01 + 1)],
                        reads=[(nm + "dna",)], writes=[(nm + "dna",)],
                        tag=nm + "drec")
                P.trans(pq[0:16, 0:128], den_asm[:, 16 * h01:16 * (h01 + 1)],
                        cs["ident_f"][:],
                        reads=[(nm + "dna",), "c_ident_f"], writes=[KQ],
                        tag=nm + "dtr")
                P.copy("v", mxT[:], pq[0:16, 0:128], reads=[KQ],
                       writes=[(nm + "mxT",)], tag=nm + "dev")
                P.dma("g", dnrow[h01][0:1, :], mxT[:], reads=[(nm + "mxT",)],
                      writes=[(nm + "dnr", h01)], tag=nm + "ddma")
            for h01 in range(2):
                P.trans(pq[0:16, 0:128], mxA[:, 16 * h01:16 * (h01 + 1)],
                        cs["ident_f"][:],
                        reads=[(nm + "mxA", h01), "c_ident_f"], writes=[KQ],
                        tag=nm + "mxtr")
                P.ts("v", mxT[:], pq[0:16, 0:128], 0.125, None, ALU.mult,
                     reads=[KQ], writes=[(nm + "mxT",)], tag=nm + "mxneg")
                P.dma("g", mxrow[h01][0:1, :], mxT[:], reads=[(nm + "mxT",)],
                      writes=[(nm + "mxr", h01)], tag=nm + "mxdma")
            if INNER_DBG < 5:
                continue
            for h01 in range(2):
                rb = 64 * h01
                h = 8 * hg + 2 * hp + h01
                ps1, ps2, k1, k2 = (pa, pa2, KA, KA2) if h01 == 0 else (pb, pb2, KB, KB2)
                for b in range(NB):
                    t0 = 256 * b
                    taus = [(t0 - 256, 128, 0, 128), (t0 - 128, 256, 128, 384),
                            (t0, 256, 384, 640), (t0 + 128, 128, 640, 768)]
                    qt_off = [0, 0, 0, 128]
                    live = [tau for tau in range(4) if taus[tau][0] >= 0]
                    for tau in live:
                        kt0, qw, c0, c1 = taus[tau]
                        ktsl = slice(kt0, kt0 + 128)
                        qsl = slice(t0 + qt_off[tau], t0 + qt_off[tau] + qw)
                        tps = ps1 if tau % 2 == 0 else ps2
                        tk = k1 if tau % 2 == 0 else k2
                        pcol = slice(0 if tau < 2 else 256,
                                     (0 if tau < 2 else 256) + qw)
                        P.mm(tps[:, pcol], kb[hp][rb:rb + 64, ktsl],
                             qb[hp][rb:rb + 64, qsl],
                             tile_position=(rb, 0), start=True, stop=False,
                             reads=qkeys + kkeys, writes=[tk], tag=nm + "mmP")
                        P.mm(tps[:, pcol], cs["ones_row_b"][0:1, :],
                             mxrow[h01][0:1, t0 + qt_off[tau]:t0 + qt_off[tau] + qw],
                             start=False, stop=True,
                             reads=[(nm + "mxr", h01), "c_ones_row_b"],
                             writes=[tk], tag=nm + "mmPmx")
                        P.act(Pts[:, c0:c1], tps[:, pcol], AF.Exp, scale=8.0,
                              reads=[tk], writes=[(nm + "Pts", tau)],
                              tag=nm + "expP")
                    if 0 in live:
                        P.tt("v", Pts[:, 0:128], Pts[:, 0:128], cs["mask_lo"][:],
                             ALU.mult, reads=[(nm + "Pts", 0), "c_mask_lo"],
                             writes=[(nm + "Pts", 0)], tag=nm + "msk0")
                        P.tt("v", Pts[:, 256:384], Pts[:, 256:384],
                             cs["mask_lo"][:], ALU.mult,
                             reads=[(nm + "Pts", 1), "c_mask_lo"],
                             writes=[(nm + "Pts", 1)], tag=nm + "msk1")
                    P.tt("v", Pts[:, 384:512], Pts[:, 384:512], cs["mask_ca"][:],
                         ALU.mult, reads=[(nm + "Pts", 2), "c_mask_ca"],
                         writes=[(nm + "Pts", 2)], tag=nm + "msk2")
                    P.tt("v", Pts[:, 640:768], Pts[:, 640:768], cs["mask_ca"][:],
                         ALU.mult, reads=[(nm + "Pts", 3), "c_mask_ca"],
                         writes=[(nm + "Pts", 3)], tag=nm + "msk3")
                    first = True
                    for tau in live:
                        kt0, qw, c0, c1 = taus[tau]
                        kt_t = kt0 // 128
                        yq = slice(qt_off[tau], qt_off[tau] + qw)
                        P.mm(py[0:64, yq], vv[kt_t][:, 64 * (h % 8):64 * (h % 8) + 64],
                             Pts[:, c0:c1], start=first, stop=(tau == live[-1]),
                             reads=[(nm + "Pts", tau), (nm + "v", kt_t)],
                             writes=[KY], tag=nm + "mmav")
                        first = False
                    P.copy("v", attnT[4 * hg + hp][rb:rb + 64, t0:t0 + 256],
                           py[0:64, 0:256], reads=[KY],
                           writes=[(nm + "at", 4 * hg + hp, h01, b)],
                           tag=nm + "yev")
                for tb in range(NTB):
                    sl = slice(TB * tb, TB * (tb + 1))
                    dps = pe if h01 == 0 else py
                    dk = KE if h01 == 0 else KY
                    P.mm(dps[0:64, 0:TB], cs["ones_row_b"][0:1, 0:64],
                         dnrow[h01][0:1, sl],
                         reads=[(nm + "dnr", h01), "c_ones_row_b"], writes=[dk],
                         tag=nm + "mmdn")
                    P.tt("v", attnT[4 * hg + hp][rb:rb + 64, sl],
                         attnT[4 * hg + hp][rb:rb + 64, sl], dps[0:64, 0:TB],
                         ALU.mult,
                         reads=[(nm + "at", 4 * hg + hp, h01, tb * 2),
                                (nm + "at", 4 * hg + hp, h01, tb * 2 + 1), dk],
                         writes=[(nm + "at", 4 * hg + hp, h01, tb * 2),
                                 (nm + "at", 4 * hg + hp, h01, tb * 2 + 1)],
                         tag=nm + "atdiv")
    P.fence()
    for t_ in ([nm + f"xb{i}" for i in reversed(range(NCT))]
               + [nm + f"v{t}" for t in reversed(range(NTT))]
               + [nm + "dnr1", nm + "dnr0", nm + "dna"]
               + [nm + f"k{i}" for i in reversed(range(4))]
               + [nm + f"q{i}" for i in reversed(range(4))]
               + [nm + "wbv"] + [nm + f"wbq{k}" for k in (1, 0)]):
        ctx.free(t_)
    wo_b = [ctx.sb(nm + f"wo{k}", [128, NCT * 128], BF16) for k in range(2)]
    xsl = [ctx.sb(nm + f"xsl{k}", [128, TB], F32) for k in range(2)]
    osl = [ctx.sb(nm + f"osl{k}", [128, TB], F32) for k in range(2)]
    xsh = [ctx.sb(f"xsh{li}_{i}", [128, T], F32, side="right")
           for i in range(NCT)]
    atkeys = lambda ci: [(nm + "at", ci, h01, b) for h01 in range(2)
                         for b in range(NB)]
    for o in range(NCT):
        wk = o % 2
        load_wcols(P, "g", wo_b[wk], wo3, 128 * o, 128, (nm + "wo", wk),
                   tag=nm + "ldwo")
        for tb in range(NTB):
            sl = slice(TB * tb, TB * (tb + 1))
            ps, pk_ = (pq, KQ) if tb % 2 == 0 else (pc, KC)
            pk2 = tb % 2
            P.dma("y", xsl[pk2][:], src[128 * o:128 * (o + 1), sl],
                  reads=[("H", srcn)], writes=[(nm + "xsl", pk2)],
                  tag=nm + "ldxi")
            for ci in range(NCT):
                P.mm(ps[:], wo_b[wk][:, 128 * ci:128 * (ci + 1)],
                     attnT[ci][:, sl],
                     start=(ci == 0), stop=(ci == NCT - 1),
                     reads=[(nm + "wo", wk)] + atkeys(ci), writes=[pk_],
                     tag=nm + "mmwo")
            P.tt("v", osl[pk2][:], ps[:], xsl[pk2][:], ALU.add,
                 reads=[pk_, (nm + "xsl", pk2)], writes=[(nm + "osl", pk2)],
                 tag=nm + "xadd")
            P.dma("y", mid[128 * o:128 * (o + 1), sl], osl[pk2][:],
                  reads=[(nm + "osl", pk2)], writes=[("H", midn)],
                  tag=nm + "stmid")
            if o < 4:
                P.dma("s", xsh[o][:, sl], osl[pk2][:],
                      reads=[(nm + "osl", pk2)], writes=[(f"xsh{li}", o)],
                      tag=nm + "xcp")
            else:
                w = TB if tb < NTB - 1 else TB - 1
                P.dma("s", xsh[o][:, TB * tb + 1:TB * tb + 1 + w],
                      osl[pk2][:, 0:w],
                      reads=[(nm + "osl", pk2)], writes=[(f"xsh{li}", o)],
                      tag=nm + "xcps")
    P.fence()
    for t_ in ([nm + f"osl{k}" for k in (1, 0)] + [nm + f"xsl{k}" for k in (1, 0)]
               + [nm + f"wo{k}" for k in (1, 0)]
               + [nm + f"sq{k}" for k in (1, 0)][::-1]):
        pass
    for t_ in ([nm + f"osl{k}" for k in (1, 0)] + [nm + f"xsl{k}" for k in (1, 0)]
               + [nm + f"wo{k}" for k in (1, 0)]):
        ctx.free(t_)
    for t_ in ([nm + f"rp{k}" for k in (1, 0)] + [nm + f"sq{k}" for k in (1, 0)]
               + [nm + "rdr", nm + "ysb", nm + "mxr1", nm + "mxr0", nm + "mxT",
                  nm + "mxA", nm + "Pts", nm + "qsc", nm + "rsk", nm + "rsq"]
               + [nm + f"at{i}" for i in reversed(range(NCT))]):
        ctx.free(t_)
    for t_ in (nm + "pe", nm + "py", nm + "pb2", nm + "pa2", nm + "pb", nm + "pa",
               nm + "pc", nm + "pq"):
        ctx.free(t_)


def emit_inner_ffn(ctx, li, mid, dst, midn="mid", dstn="dst"):
    P, cs = ctx.P, ctx.consts
    nm = f"i{li}f_"
    W1_h = ctx.dram("L1_W1", [6, C, 5460], F32).ap()
    W2_h = ctx.dram("L1_W2", [6, DIM_INNER, C], F32).ap()
    b1_h = ctx.dram("L1_b1", [6, 5460], F32).ap()
    b2_h = ctx.dram("L1_b2", [6, 128, NCT], F32).ap()
    lng_h = ctx.dram("L1_lng", [6, 128, NCT], F32).ap()
    lnb_h = ctx.dram("L1_lnb", [6, 128, NCT], F32).ap()
    w13 = W1_h[li].rearrange("(a p) o -> p a o", p=128)
    NJ = len(INNER_JT)

    gb = ctx.sb(nm + "gb", [128, 4 * NCT], F32)
    b1c = ctx.sb(nm + "b1c", [128, 2 * NJ], F32)
    xnf = [ctx.sb(nm + f"xn{i}", [128, T], BF16) for i in range(NCT)]
    alloc_ln_scratch(ctx, nm + "ln_")
    xs = [ctx.tiles[f"xsh{li}_{i}"] for i in range(NCT)]
    P.dma("y", gb[:, 0:NCT], lng_h[li], writes=[(nm + "gb",)], tag=nm + "ldg")
    P.dma("y", gb[:, NCT:2 * NCT], lnb_h[li], writes=[(nm + "gb",)],
          tag=nm + "ldb")
    P.dma("y", gb[:, 2 * NCT:3 * NCT], b2_h[li], writes=[(nm + "gb",)],
          tag=nm + "ldb2")
    off = 0
    for j, jw in enumerate(INNER_JT):
        P.dma("y", b1c[0:jw, j:j + 1], b1_h[li, off:off + jw],
              writes=[(nm + "b1c",)], tag=nm + "ldb1a")
        P.dma("y", b1c[0:jw, NJ + j:NJ + j + 1],
              b1_h[li, DIM_INNER + off:DIM_INNER + off + jw],
              writes=[(nm + "b1c",)], tag=nm + "ldb1g")
        off += jw
    for i in range(4, NCT):
        P.memset("g", xs[i][:, 0:1], 0.0, writes=[(f"xsh{li}", i)],
                 tag=nm + "xsz")
    emit_layernorm(ctx, xs, f"xsh{li}", xnf, nm + "xn", nm + "ln_",
                   gamma=[gb[:, i:i + 1] for i in range(NCT)],
                   beta=[gb[:, NCT + i:NCT + i + 1] for i in range(NCT)],
                   gkey=(nm + "gb",))
    P.fence()
    for i in reversed(range(NCT)):
        ctx.free(f"xsh{li}_{i}")
    for t_ in (nm + "ln_rstdB", nm + "ln_muB", nm + "ln_t1", nm + "ln_rstd",
               nm + "ln_mu", nm + "ln_sq"):
        ctx.free(t_)
    for t_ in (nm + "ln_bc", nm + "ln_ps1", nm + "ln_ps0"):
        ctx.free(t_)
    xbn = None
    if li + 1 < 6:
        xbn = [ctx.sb(f"i{li + 1}a_xb{i}", [128, T], BF16, side="right")
               for i in range(NCT)]
    hb = [ctx.sb(nm + f"hb{j}", [128, 2 * TB], BF16) for j in range(NJ)]
    w1a = [ctx.sb(nm + f"w1a{k}", [128, NCT * 128], BF16) for k in range(2)]
    w1g = [ctx.sb(nm + f"w1g{k}", [128, NCT * 128], BF16) for k in range(2)]
    w2b = [ctx.sb(nm + f"w2b{j}", [128, C], BF16) for j in range(NJ)]
    scr1 = ctx.sb(nm + "scr1", [128, TB], F32)
    scr2 = ctx.sb(nm + "scr2", [128, TB], F32)
    xsl = [ctx.sb(nm + f"xsl{k}", [128, TB], F32) for k in range(2)]
    pg = ctx.ps(nm + "pg", [128, 512], F32)
    pu = ctx.ps(nm + "pu", [128, 512], F32)
    po = [ctx.ps(nm + f"po{k}", [128, 512], F32) for k in range(2)]
    KG, KU = (nm + "PG",), (nm + "PU",)
    xnkeys = [(nm + "xn", i) for i in range(NCT)]
    OFFS = []
    off = 0
    for jw in INNER_JT:
        OFFS.append(off)
        off += jw
    def w1_load(j):
        wk = j % 2
        load_wcols(P, "g", w1a[wk], w13, OFFS[j], INNER_JT[j], (nm + "w1a", wk),
                   tag=nm + "ldw1a")
        load_wcols(P, "g", w1g[wk], w13, DIM_INNER + OFFS[j], INNER_JT[j],
                   (nm + "w1g", wk), tag=nm + "ldw1g")
    w1_load(0)
    for th in range(2):
        hsl = slice(1024 * th, 1024 * (th + 1))
        for j, jw in enumerate(INNER_JT):
            wk = j % 2
            if j + 1 < NJ:
                w1_load(j + 1)
            elif th == 0:
                w1_load(0)
            if th == 0:
                P.dma("g", w2b[j][0:jw, :], W2_h[li, OFFS[j]:OFFS[j] + jw, :],
                      writes=[(nm + "w2b", j)], tag=nm + "ldw2")
            for sb2 in range(2):
                sl = slice(1024 * th + 512 * sb2, 1024 * th + 512 * (sb2 + 1))
                osl2 = slice(512 * sb2, 512 * (sb2 + 1))
                pg_, pu_, KG_, KU_ = (pg, pu, KG, KU) if sb2 == 0 else \
                    (po[0], po[1], (nm + "po", 0), (nm + "po", 1))
                for ci in range(NCT):
                    P.mm(pg_[0:jw, :], w1a[wk][:, jw * ci:jw * (ci + 1)],
                         xnf[ci][:, sl], start=(ci == 0), stop=(ci == NCT - 1),
                         reads=[(nm + "w1a", wk)] + xnkeys, writes=[KG_],
                         tag=nm + "mmA")
                for ci in range(NCT):
                    P.mm(pu_[0:jw, :], w1g[wk][:, jw * ci:jw * (ci + 1)],
                         xnf[ci][:, sl], start=(ci == 0), stop=(ci == NCT - 1),
                         reads=[(nm + "w1g", wk)] + xnkeys, writes=[KU_],
                         tag=nm + "mmG")
                scr = scr1 if sb2 == 0 else scr2
                srk = (nm + "scr1",) if sb2 == 0 else (nm + "scr2",)
                P.act(scr[0:jw, :], pu_[0:jw, :], AF.Gelu,
                      bias=b1c[0:jw, NJ + j:NJ + j + 1],
                      reads=[KU_, (nm + "b1c",)], writes=[srk], tag=nm + "gelu")
                P.stt("v", hb[j][0:jw, osl2], pg_[0:jw, :],
                      b1c[0:jw, j:j + 1], scr[0:jw, :], ALU.add, ALU.mult,
                      reads=[KG_, (nm + "b1c",), srk], writes=[(nm + "hb", j)],
                      tag=nm + "hmul")
            off += jw
        for o in range(NCT):
            for sb2 in range(2):
                sl = slice(1024 * th + 512 * sb2, 1024 * th + 512 * (sb2 + 1))
                osl2 = slice(512 * sb2, 512 * (sb2 + 1))
                pk2 = (2 * o + sb2) % 2
                P.dma("y", xsl[pk2][:], mid[128 * o:128 * (o + 1), sl],
                      reads=[("H", midn)], writes=[(nm + "xsl", pk2)],
                      tag=nm + "ldxi")
                for j, jw in enumerate(INNER_JT):
                    P.mm(po[pk2][:], w2b[j][0:jw, 128 * o:128 * (o + 1)],
                         hb[j][0:jw, osl2], start=(j == 0), stop=(j == NJ - 1),
                         reads=[(nm + "w2b", j), (nm + "hb", j)],
                         writes=[(nm + "po", pk2)], tag=nm + "mmW2")
                scr = scr1 if pk2 == 0 else scr2
                srk = (nm + "scr1",) if pk2 == 0 else (nm + "scr2",)
                P.ts("v", scr[:], po[pk2][:], gb[:, 2 * NCT + o:2 * NCT + o + 1],
                     None, ALU.add, reads=[(nm + "po", pk2), (nm + "gb",)],
                     writes=[srk], tag=nm + "b2add")
                P.tt("v", scr[:], scr[:], xsl[pk2][:], ALU.add,
                     reads=[srk, (nm + "xsl", pk2)], writes=[srk],
                     tag=nm + "xadd")
                P.dma("y", dst[128 * o:128 * (o + 1), sl], scr[:],
                      reads=[srk], writes=[("H", dstn)], tag=nm + "stout")
                if xbn is not None:
                    c0 = 1024 * th + 512 * sb2
                    if o < 4:
                        P.dma("g", xbn[o][:, sl], scr[:], reads=[srk],
                              writes=[(f"i{li + 1}a_xb", o)], tag=nm + "xbh")
                    else:
                        w = TB if c0 + TB < T else TB - 1
                        P.dma("g", xbn[o][:, c0 + 1:c0 + 1 + w], scr[:, 0:w],
                              reads=[srk], writes=[(f"i{li + 1}a_xb", o)],
                              tag=nm + "xbhs")
    P.fence()
    for t_ in ([nm + f"po{k}" for k in (1, 0)] + [nm + "pu", nm + "pg"]):
        ctx.free(t_)
    for t_ in ([nm + f"xsl{k}" for k in (1, 0)] + [nm + "scr2", nm + "scr1"]
               + [nm + f"w2b{j}" for j in reversed(range(NJ))]
               + [nm + f"w1g{k}" for k in (1, 0)]
               + [nm + f"w1a{k}" for k in (1, 0)]
               + [nm + f"hb{j}" for j in reversed(range(NJ))]):
        ctx.free(t_)
    for t_ in [nm + f"xn{i}" for i in reversed(range(NCT))]:
        ctx.free(t_)
    for t_ in (nm + "b1c", nm + "gb"):
        ctx.free(t_)


def stage_inner(ctx, li, src_name, dst_name):
    ctx.need_consts(["ones_col", "ones_row", "ones_row_b", "ident_f", "in_perm", "in_perm_f",
                     "cosq", "sinq", "cosk", "sink", "hmask", "bcmask", "mask_lo",
                     "mask_ca", "mband", "mq1", "mq0"])
    src = ctx.dram(src_name, [C, T], F32).ap()
    dst = ctx.dram(dst_name, [C, T], F32).ap()
    mid = ctx.dram(f"xmid{li}", [C, T], F32).ap()
    emit_inner_attn(ctx, li, src, mid, src_name, f"xmid{li}")
    emit_inner_ffn(ctx, li, mid, dst, f"xmid{li}", dst_name)



def stage_final(ctx, xi_name="xi_h"):
    P = ctx.P
    x1 = ctx.dram("x1_h", [C, T], F32).ap()
    s2 = ctx.dram("s2_h", [C, T], F32).ap()
    xi = ctx.dram(xi_name, [C, T], F32).ap()
    out = ctx.dram("out_h", [C, T], F32).ap()
    a = [ctx.sb(f"fn_a{k}", [128, T], F32) for k in range(2)]
    b = [ctx.sb(f"fn_b{k}", [128, T], F32) for k in range(2)]
    c = [ctx.sb(f"fn_c{k}", [128, T], F32) for k in range(2)]
    for i in range(NCT):
        pk = i % 2
        P.dma("y", a[pk][:], xi[128 * i:128 * (i + 1), :],
              reads=[("H", xi_name)], writes=[("fn_a", pk)], tag="fn_ldxi")
        P.dma("y", b[pk][:], s2[128 * i:128 * (i + 1), :],
              reads=[("H", "s2_h")], writes=[("fn_b", pk)], tag="fn_lds2")
        P.dma("y", c[pk][:], x1[128 * i:128 * (i + 1), :],
              reads=[("H", "x1_h")], writes=[("fn_c", pk)], tag="fn_ldx1")
        P.act(a[pk][:], a[pk][:], AF.Silu, reads=[("fn_a", pk)],
              writes=[("fn_a", pk)], tag="fn_silu")
        P.tt("v", a[pk][:], a[pk][:], b[pk][:], ALU.mult,
             reads=[("fn_a", pk), ("fn_b", pk)], writes=[("fn_a", pk)],
             tag="fn_mul")
        P.tt("v", a[pk][:], a[pk][:], c[pk][:], ALU.add,
             reads=[("fn_a", pk), ("fn_c", pk)], writes=[("fn_a", pk)],
             tag="fn_add")
        P.dma("y", out[128 * i:128 * (i + 1), :], a[pk][:],
              reads=[("fn_a", pk)], writes=[("H", "out_h")], tag="fn_st")



def build_program(stage_fns, const_arrays, extra_inputs, outputs_spec):
    from contextlib import ExitStack
    nc = bass.Bass("TRN2", target_bir_lowering=False)
    es = ExitStack()
    P = Prog()
    ctx = Ctx(nc, P, es)
    ctx.const_arrays = const_arrays
    ctx.consts = {}
    ctx.used_consts = []
    ctx.need_consts(["scal"])
    for name, (shape, dt) in extra_inputs.items():
        ctx.dram(name, shape, dt, kind="ExternalInput")
    for name, (shape, dt) in outputs_spec.items():
        ctx.dram(name, shape, dt, kind="ExternalOutput")
    for fn in stage_fns:
        fn(ctx)
    sems = {}
    for e in ENGINES:
        sems[(e, "c")] = es.enter_context(nc.semaphore("sem_" + e + "c"))
    for e in DMA_QUEUES:
        for s in range(Prog.NS):
            sems[(e, s)] = es.enter_context(nc.semaphore(f"sem_{e}d{s}"))
    nums = sorted(s.num for s in sems.values())
    rng = range(nums[0], nums[-1] + 1)
    nc.gpsimd.dma_reset(rng)
    nc.gpsimd.sem_clear(rng)
    nc.all_engine_barrier()
    with nc.Block() as block:
        P.emit(nc, block, sems)
    es.close()
    nc.reset()
    return nc, P, ctx


_PROG_CACHE = {}


def _get_prog(name, stage_fns, extra_in, out_spec):
    if name not in _PROG_CACHE:
        consts = make_consts()
        nc, P, ctx = build_program(stage_fns, consts, extra_in, out_spec)
        cfeeds = {"c_" + k: np.ascontiguousarray(consts[k])
                  for k in ctx.used_consts}
        _PROG_CACHE[name] = (nc, cfeeds)
    return _PROG_CACHE[name]


ACT_SPEC = ([C, T], F32)


def _prep_host(inputs):
    h = {}
    ct = np.ascontiguousarray
    h["w_inT"] = ct(inputs["L0_attn_w_in"].T)
    h["w_outT"] = ct(inputs["L0_attn_w_out"].T)
    h["l0_gamma"] = ct(np.asarray(inputs["L0_attn_ln_gamma"]).reshape(NCT, 128).T)
    h["l0_beta"] = ct(np.asarray(inputs["L0_attn_ln_beta"]).reshape(NCT, 128).T)
    h["wgT"] = ct(inputs["L0_ffn_w_gate"].T)
    h["wuT"] = ct(inputs["L0_ffn_w_up"].T)
    h["wdT"] = ct(inputs["L0_ffn_w_down"].T)
    par = np.zeros((128, 5 * NCT), np.float32)
    for i in range(NCT):
        par[:, i] = inputs["L1_attn_time_decay"][128 * i:128 * (i + 1)]
        par[:, NCT + i] = inputs["L1_attn_time_first"][128 * i:128 * (i + 1)]
        par[:, 2 * NCT + i] = inputs["L1_attn_time_mix_k"][0, 0][128 * i:128 * (i + 1)]
        par[:, 3 * NCT + i] = inputs["L1_attn_time_mix_v"][0, 0][128 * i:128 * (i + 1)]
        par[:, 4 * NCT + i] = inputs["L1_attn_time_mix_r"][0, 0][128 * i:128 * (i + 1)]
    h["rwkv_par"] = par
    h["WkT"] = ct(inputs["L1_attn_Wk"].T)
    h["WvT"] = ct(inputs["L1_attn_Wv"].T)
    h["WrT"] = ct(inputs["L1_attn_Wr"].T)
    h["WoT"] = ct(inputs["L1_attn_Wo"].T)
    h["L1_Wqkv"] = ct(inputs["L1_ffn_Wqkv"])
    h["L1_Wo"] = ct(inputs["L1_ffn_Wo"])
    h["L1_qs"] = ct(np.tile(np.asarray(inputs["L1_ffn_q_scale"]), (1, 2))[:, :, None])
    h["L1_ks"] = ct(np.tile(np.asarray(inputs["L1_ffn_k_scale"]), (1, 2))[:, :, None])
    h["L1_W1"] = ct(inputs["L1_ffn_W1"])
    h["L1_W2"] = ct(inputs["L1_ffn_W2"])
    h["L1_b1"] = ct(inputs["L1_ffn_b1"])
    h["L1_b2"] = ct(np.asarray(inputs["L1_ffn_b2"]).reshape(6, NCT, 128).transpose(0, 2, 1))
    h["L1_lng"] = ct(np.asarray(inputs["L1_ffn_ln_g"]).reshape(6, NCT, 128).transpose(0, 2, 1))
    h["L1_lnb"] = ct(np.asarray(inputs["L1_ffn_ln_b"]).reshape(6, NCT, 128).transpose(0, 2, 1))
    return h


IN_SPECS = {
    "w_inT": ([C, 3 * C], F32), "w_outT": ([C, C], F32),
    "l0_gamma": ([128, NCT], F32), "l0_beta": ([128, NCT], F32),
    "wgT": ([C, 3072], I32), "wuT": ([C, 3072], I32), "wdT": ([3072, C], I32),
    "rwkv_par": ([128, 5 * NCT], F32),
    "WkT": ([C, C], F32), "WvT": ([C, C], F32), "WrT": ([C, C], F32),
    "WoT": ([C, C], F32),
    "L1_Wqkv": ([6, C, 3 * C], F32), "L1_Wo": ([6, C, C], F32),
    "L1_qs": ([6, 128, 1], F32), "L1_ks": ([6, 128, 1], F32),
    "L1_W1": ([6, C, 5460], F32), "L1_W2": ([6, DIM_INNER, C], F32),
    "L1_b1": ([6, 5460], F32), "L1_b2": ([6, 128, NCT], F32),
    "L1_lng": ([6, 128, NCT], F32), "L1_lnb": ([6, 128, NCT], F32),
}


def _spec_sub(h, names):
    return {n: IN_SPECS[n] for n in names}


FUSED = True


def _fused_stages():
    fns = [
        lambda ctx: stage_ln(ctx, "x_in", "xn_h", pref="ln1_"),
        stage_l0_attn,
        lambda ctx: (ctx.free_const("l0_cos"), ctx.free_const("l0_sin"),
                     stage_l0_ffn_combine(ctx)),
        lambda ctx: stage_ln(ctx, "x1_h", "xn2_h", pref="ln2_"),
        stage_rwkv,
    ]
    for li in range(6):
        srcn = "xn2_h" if li == 0 else f"xi{li - 1}_h"
        fns.append(lambda ctx, li=li, s=srcn: stage_inner(ctx, li, s, f"xi{li}_h"))
    fns.append(lambda ctx: stage_final(ctx, xi_name="xi5_h"))
    return fns


def kernel_fused(inputs):
    B = 8
    host = _prep_host(inputs)
    x = np.asarray(inputs["L0_attn_x"])
    xT = [np.ascontiguousarray(x[b].T) for b in range(B)]
    extra_in = dict(IN_SPECS)
    extra_in["x_in"] = ACT_SPEC
    nc, cfeeds = _get_prog("fused", _fused_stages(), extra_in,
                           {"out_h": ACT_SPEC})
    in_maps = []
    for b in range(B):
        m = dict(cfeeds)
        for k in IN_SPECS:
            m[k] = host[k]
        m["x_in"] = xT[b]
        in_maps.append(m)
    res = run_bass_kernel_spmd(nc, in_maps, core_ids=list(range(B)))
    out = np.stack([res.results[b]["out_h"].T for b in range(B)], axis=0)
    return np.ascontiguousarray(out.astype(np.float32))


def kernel(**inputs):
    if FUSED:
        return kernel_fused(inputs)
    B = 8
    host = _prep_host(inputs)
    x = np.asarray(inputs["L0_attn_x"])
    xT = [np.ascontiguousarray(x[b].T) for b in range(B)]

    launches = [
        ("ln1", [lambda ctx: stage_ln(ctx, "x_in", "xn_h")],
         {"x_in": ACT_SPEC}, {"xn_h": ACT_SPEC}, ["x_in"], []),
        ("l0attn", [stage_l0_attn],
         dict(_spec_sub(host, ["w_inT", "w_outT", "l0_gamma", "l0_beta"]),
              xn_h=ACT_SPEC),
         {"s1_h": ACT_SPEC}, ["xn_h"],
         ["w_inT", "w_outT", "l0_gamma", "l0_beta"]),
        ("l0ffn", [stage_l0_ffn_combine],
         dict(_spec_sub(host, ["wgT", "wuT", "wdT"]), xn_h=ACT_SPEC,
              x_in=ACT_SPEC, s1_h=ACT_SPEC),
         {"x1_h": ACT_SPEC}, ["xn_h", "x_in", "s1_h"], ["wgT", "wuT", "wdT"]),
        ("ln2", [lambda ctx: stage_ln(ctx, "x1_h", "xn2_h")],
         {"x1_h": ACT_SPEC}, {"xn2_h": ACT_SPEC}, ["x1_h"], []),
        ("rwkv", [stage_rwkv],
         dict(_spec_sub(host, ["WkT", "WvT", "WrT", "WoT", "rwkv_par"]),
              xn2_h=ACT_SPEC),
         {"s2_h": ACT_SPEC}, ["xn2_h"],
         ["WkT", "WvT", "WrT", "WoT", "rwkv_par"]),
    ]
    innames = ["L1_Wqkv", "L1_Wo", "L1_qs", "L1_ks", "L1_W1", "L1_W2",
               "L1_b1", "L1_b2", "L1_lng", "L1_lnb"]
    for li in range(6):
        src = "xn2_h" if li == 0 else f"xi{li - 1}_h"
        dst = f"xi{li}_h"
        launches.append(
            (f"inner{li}",
             [lambda ctx, li=li, src=src, dst=dst: stage_inner(ctx, li, src, dst)],
             dict(_spec_sub(host, innames), **{src: ACT_SPEC}),
             {dst: ACT_SPEC}, [src], innames))
    launches.append(
        ("final", [stage_final],
         {"x1_h": ACT_SPEC, "s2_h": ACT_SPEC, "xi_h": ACT_SPEC},
         {"out_h": ACT_SPEC}, ["x1_h", "s2_h", "xi_h"], []))

    state = [{"x_in": xT[b]} for b in range(B)]
    for (name, fns, ins, outs, act_ins, w_ins) in launches:
        nc, cfeeds = _get_prog(name, fns, ins, outs)
        in_maps = []
        for b in range(B):
            m = dict(cfeeds)
            for k in w_ins:
                m[k] = host[k]
            for k in act_ins:
                kk = "xi_h" if (name == "final" and k == "xi_h") else k
                m[k] = state[b][k]
            in_maps.append(m)
        res = run_bass_kernel_spmd(nc, in_maps, core_ids=list(range(B)))
        for b in range(B):
            for k in outs:
                state[b][k] = np.ascontiguousarray(res.results[b][k])
        if name == "inner5":
            for b in range(B):
                state[b]["xi_h"] = state[b]["xi5_h"]
    out = np.stack([state[b]["out_h"].T for b in range(B)], axis=0)
    return np.ascontiguousarray(out.astype(np.float32))
